# version 15
# baseline (speedup 1.0000x reference)
import numpy as np
from contextlib import ExitStack
import concourse.bass as bass
import concourse.mybir as mybir
from concourse.bass_utils import run_bass_kernel_spmd

F32 = mybir.dt.float32
BF16 = mybir.dt.bfloat16
AF = mybir.ActivationFunctionType
ALU = mybir.AluOpType
AX = mybir.AxisListType

NEG = -1e38


class Cfg:
    def __init__(self, C=1024, H=4096, E=8, T=4096, S=2, TBLK=512, SBB=2, TC=128,
                 HQ=1024):
        self.C, self.H, self.E, self.T, self.S = C, H, E, T, S
        self.TBLK, self.SBB, self.TC, self.HQ = TBLK, SBB, TC, HQ
        self.CC = C // 128
        self.TPB = TBLK // 128
        self.HCQ = HQ // 128
        self.QPE = H // HQ
        self.NQ = E * self.QPE
        self.HCALL = H // 128
        self.TOK = S * T
        self.NBLK = self.TOK // TBLK
        self.NSB = self.NBLK // SBB
        self.NCH = T // TC
        self.ND = max(1, C // 512)
        self.DN = min(512, C)
        self.G = S * self.CC
        assert C % 128 == 0 and H % 128 == 0 and TBLK % 128 == 0
        assert self.TOK % TBLK == 0 and self.NBLK % SBB == 0 and T % TC == 0
        assert TC == 128 and HQ % 128 == 0 and H % HQ == 0 and C % self.DN == 0


def _ln_blk_scalar(s, P, x_aps, blk, NT, sq, st, lntok2, s_ls, s_lv, s_lr,
                   s_lm, s_la, s_tp):
    for tt in range(NT):
        i = blk * NT + tt
        if i > 0 or tt > 0:
            s.drain()
        s.activation(out=sq[:, :], in_=x_aps[tt], func=AF.Square, bias=0.0,
                     scale=1.0, accum_out=st[:, tt, 1:2]).then_inc(s_ls, 1)
    s.wait_ge(s_lv, blk + 1)
    s.activation(out=st[:, :, 3:4], in_=st[:, :, 3:4], func=AF.Ln,
                 bias=P["eps_col"][:, :], scale=1.0)
    s.drain()
    s.activation(out=st[:, :, 3:4], in_=st[:, :, 3:4], func=AF.Exp,
                 bias=0.0, scale=-0.5).then_inc(s_lr, 1)
    s.wait_ge(s_lm, blk + 1)
    for tt in range(NT):
        i = blk * NT + tt
        if i >= 2:
            s.wait_ge(s_tp, i - 1)
        s.activation(out=lntok2[:, i % 2, :], in_=x_aps[tt], func=AF.Identity,
                     scale=st[:, tt, 3:4],
                     bias=st[:, tt, 4:5]).then_inc(s_la, 1)


def _ln_blk_vector(v, Cinv, x_aps, blk, NT, st, s_ls, s_lv, s_lr, s_lm):
    v.drain()
    for tt in range(NT):
        v.tensor_reduce(out=st[:, tt, 0:1], in_=x_aps[tt], axis=AX.X,
                        op=ALU.add)
    v.wait_ge(s_ls, blk * NT + NT)
    v.drain()
    v.tensor_scalar(out=st[:, :, 2:3], in0=st[:, :, 0:1], scalar1=Cinv,
                    scalar2=None, op0=ALU.mult)
    v.tensor_scalar(out=st[:, :, 3:4], in0=st[:, :, 1:2], scalar1=Cinv,
                    scalar2=None, op0=ALU.mult)
    v.drain()
    v.tensor_tensor(out=st[:, :, 4:5], in0=st[:, :, 2:3], in1=st[:, :, 2:3],
                    op=ALU.mult)
    v.drain()
    v.tensor_tensor(out=st[:, :, 3:4], in0=st[:, :, 3:4], in1=st[:, :, 4:5],
                    op=ALU.subtract).then_inc(s_lv, 1)
    v.wait_ge(s_lr, blk + 1)
    v.tensor_tensor(out=st[:, :, 4:5], in0=st[:, :, 2:3], in1=st[:, :, 3:4],
                    op=ALU.mult)
    v.drain()
    v.tensor_scalar(out=st[:, :, 4:5], in0=st[:, :, 4:5], scalar1=-1.0,
                    scalar2=None, op0=ALU.mult).then_inc(s_lm, 1)


def emit_ffn(nc, cfg, P, x_in, x_out, w1c, w2c, label, cache_sem=None):
    C, E, H = cfg.C, cfg.E, cfg.H
    CC, TPB, HCQ, NQ, SBB, NSB, TBLK = (cfg.CC, cfg.TPB, cfg.HCQ, cfg.NQ,
                                        cfg.SBB, cfg.NSB, cfg.TBLK)
    ND, DN, HQ, QPE = cfg.ND, cfg.DN, cfg.HQ, cfg.QPE
    Cinv = 1.0 / C
    n_o2_per_q = SBB * TPB * ND

    with ExitStack() as ctx:
        E_ = ctx.enter_context
        ln_chan = E_(nc.sbuf_tensor(f"{label}_lnch",
                                    [128, 2, SBB, CC, TBLK], BF16))
        comb = E_(nc.sbuf_tensor(f"{label}_comb", [128, SBB, TPB, C], F32))
        w1q = E_(nc.sbuf_tensor(f"{label}_w1q", [128, 2, CC, HQ], BF16))
        w2q = E_(nc.sbuf_tensor(f"{label}_w2q", [128, 2, HCQ, C], BF16))
        h_sb = E_(nc.sbuf_tensor(f"{label}_hsb", [128, HCQ, TBLK], BF16))
        xa = E_(nc.sbuf_tensor(f"{label}_xa", [128, SBB * TPB, C], F32))
        xb = E_(nc.sbuf_tensor(f"{label}_xb", [128, TPB, C], F32))
        w_tok = E_(nc.sbuf_tensor(f"{label}_wtok", [128, 2, SBB, TPB, E], F32))
        w_ebf = E_(nc.sbuf_tensor(f"{label}_webf", [E, SBB, TBLK], BF16))
        gateF = E_(nc.sbuf_tensor(f"{label}_gateF", [E, TBLK], F32))
        maskF = E_(nc.sbuf_tensor(f"{label}_maskF", [E, TBLK], F32))
        sqscr = E_(nc.sbuf_tensor(f"{label}_sqscr", [128, C], BF16))
        lntok2 = E_(nc.sbuf_tensor(f"{label}_lntok", [128, 2, C], BF16))
        lnst = E_(nc.sbuf_tensor(f"{label}_lnst", [128, SBB * TPB, 5], F32))

        h_ps = [E_(nc.psum_tensor(f"{label}_hps{i}", [128, TBLK], F32))
                for i in range(2)]
        o2_ps = [E_(nc.psum_tensor(f"{label}_o2ps{i}", [128, DN], F32))
                 for i in range(2)]
        gate_ps = E_(nc.psum_tensor(f"{label}_gps", [128, TBLK], F32))
        tp_ps = E_(nc.psum_tensor(f"{label}_tpps", [128, C], BF16))
        bias_ps = E_(nc.psum_tensor(f"{label}_bps", [128, C], F32))

        s_xa = E_(nc.semaphore(f"{label}_sxa"))
        s_xb = E_(nc.semaphore(f"{label}_sxb"))
        s_w = [E_(nc.semaphore(f"{label}_sw{i}")) for i in range(2)]
        s_ls = E_(nc.semaphore(f"{label}_sls"))
        s_lv = E_(nc.semaphore(f"{label}_slv"))
        s_lr = E_(nc.semaphore(f"{label}_slr"))
        s_lm = E_(nc.semaphore(f"{label}_slm"))
        s_la = E_(nc.semaphore(f"{label}_sla"))
        s_tp = E_(nc.semaphore(f"{label}_stp"))
        s_ln = E_(nc.semaphore(f"{label}_sln"))
        s_gm = E_(nc.semaphore(f"{label}_sgm"))
        s_ge = E_(nc.semaphore(f"{label}_sge"))
        s_gw = E_(nc.semaphore(f"{label}_sgw"))
        s_wt = E_(nc.semaphore(f"{label}_swt"))
        s_wv = E_(nc.semaphore(f"{label}_swv"))
        s_bm = E_(nc.semaphore(f"{label}_sbm"))
        s_bi = E_(nc.semaphore(f"{label}_sbi"))
        s_hmm = E_(nc.semaphore(f"{label}_shmm"))
        s_hev = E_(nc.semaphore(f"{label}_shev"))
        s_o2m = E_(nc.semaphore(f"{label}_so2m"))
        s_o2f = E_(nc.semaphore(f"{label}_so2f"))
        s_xn = E_(nc.semaphore(f"{label}_sxn"))
        s_xo = E_(nc.semaphore(f"{label}_sxo"))

        def tok0(sb, blk):
            return (sb * SBB + blk) * TBLK

        def body_sync(sync):
            nxo = 0

            def loads(sb):
                if sb > 0:
                    sync.wait_ge(s_la, sb * SBB * TPB)
                sync.dma_start(
                    out=xa[:, :, :],
                    in_=x_in[tok0(sb, 0):tok0(sb, 0) + SBB * TBLK, :]
                    .rearrange("(tt p) c -> p tt c", p=128),
                ).then_inc(s_xa, 16)

            loads(0)
            for sb in range(NSB):
                for q in range(NQ):
                    L = sb * NQ + q
                    e, qi = q // QPE, q % QPE
                    if cache_sem is not None and sb == 0:
                        sync.wait_ge(cache_sem[e], 32)
                    if L >= 2:
                        sync.wait_ge(s_o2f, (L - 1) * n_o2_per_q)
                    slot = L % 2
                    sync.dma_start(
                        out=w1q[:, slot, :, :],
                        in_=w1c[e, :, qi * HQ:(qi + 1) * HQ]
                        .rearrange("(cc p) h -> p cc h", p=128),
                    ).then_inc(s_w[slot], 16)
                    sync.dma_start(
                        out=w2q[:, slot, :, :],
                        in_=w2c[e, qi * HQ:(qi + 1) * HQ, :]
                        .rearrange("(hc p) c -> p hc c", p=128),
                    ).then_inc(s_w[slot], 16)
                    if sb + 1 < NSB and q == max(0, NQ - 6):
                        loads(sb + 1)
                for blk in range(SBB):
                    gb = sb * SBB + blk
                    sync.wait_ge(s_o2f, (sb + 1) * NQ * n_o2_per_q)
                    if nxo > 0:
                        sync.wait_ge(s_xo, 16 * nxo)
                    sync.dma_start(
                        out=xb[:, :, :],
                        in_=x_in[tok0(sb, blk):tok0(sb, blk) + TBLK, :]
                        .rearrange("(tt p) c -> p tt c", p=128),
                    ).then_inc(s_xb, 16)
                    sync.wait_ge(s_xn, gb + 1)
                    sync.dma_start(
                        out=x_out[tok0(sb, blk):tok0(sb, blk) + TBLK, :]
                        .rearrange("(tt p) c -> p tt c", p=128),
                        in_=xb[:, :, :],
                    ).then_inc(s_xo, 16)
                    nxo += 1
            sync.wait_ge(s_xo, 16 * nxo)

        def body_scalar(s):
            NT = SBB * TPB

            def st_sq(sb, half):
                if half == 0:
                    s.wait_ge(s_xa, 16 * (sb + 1))
                for j in range(half * (NT // 2), (half + 1) * (NT // 2)):
                    s.drain()
                    s.activation(out=sqscr[:, :], in_=xa[:, j, :],
                                 func=AF.Square, bias=0.0, scale=1.0,
                                 accum_out=lnst[:, j, 1:2]).then_inc(s_ls, 1)

            def st_rstd(sb):
                s.wait_ge(s_lv, sb + 1)
                s.drain()
                s.activation(out=lnst[:, :, 3:4], in_=lnst[:, :, 3:4],
                             func=AF.Ln, bias=P["eps_col"][:, :], scale=1.0)
                s.drain()
                s.activation(out=lnst[:, :, 3:4], in_=lnst[:, :, 3:4],
                             func=AF.Exp, bias=0.0,
                             scale=-0.5).then_inc(s_lr, 1)

            def st_apply(sb):
                s.wait_ge(s_lm, sb + 1)
                for j in range(NT):
                    i = sb * NT + j
                    if i >= 2:
                        s.wait_ge(s_tp, i - 1)
                    s.activation(out=lntok2[:, i % 2, :], in_=xa[:, j, :],
                                 func=AF.Identity, scale=lnst[:, j, 3:4],
                                 bias=lnst[:, j, 4:5]).then_inc(s_la, 1)

            def st_gexp(sb):
                for blk in range(SBB):
                    gb = sb * SBB + blk
                    s.wait_ge(s_gm, gb + 1)
                    if gb > 0:
                        s.wait_ge(s_wt, gb * TPB)
                    s.drain()
                    s.activation(out=gateF[:, :], in_=gate_ps[0:E, :],
                                 func=AF.Exp, scale=-1.0,
                                 bias=P["negbg_col"][0:E, :]).then_inc(s_ge, 1)

            for sb in range(NSB):
                if sb == 0:
                    st_sq(0, 0)
                    st_sq(0, 1)
                    st_rstd(0)
                    st_apply(0)
                    st_gexp(0)
                for q in range(NQ):
                    L = sb * NQ + q
                    e, qi = q // QPE, q % QPE
                    for blk in range(SBB):
                        r = (sb * NQ + q) * SBB + blk
                        if r > 0:
                            s.wait_ge(s_o2m, r * TPB * ND)
                        for hc in range(HCQ):
                            gi = r * HCQ + hc
                            s.wait_ge(s_hmm, gi + 1)
                            s.activation(
                                out=h_sb[:, hc, :], in_=h_ps[gi % 2][:, :],
                                func=AF.Relu,
                                bias=P["b1_cols"][:, e,
                                                  qi * HCQ + hc:
                                                  qi * HCQ + hc + 1],
                                scale=1.0,
                            ).then_inc(s_hev, 1)
                    if sb + 1 < NSB:
                        if q == max(0, NQ - 6):
                            st_sq(sb + 1, 0)
                        if q == max(0, NQ - 5):
                            st_sq(sb + 1, 1)
                        if q == max(1, NQ - 4):
                            st_rstd(sb + 1)
                        if q == max(2, NQ - 3):
                            st_apply(sb + 1)
                        if q == NQ - 1:
                            st_gexp(sb + 1)

        def body_vector(v):
            NT = SBB * TPB

            def st_vstat(sb, half):
                if half == 0:
                    v.wait_ge(s_xa, 16 * (sb + 1))
                    v.drain()
                    for j in range(NT // 2):
                        v.tensor_reduce(out=lnst[:, j, 0:1], in_=xa[:, j, :],
                                        axis=AX.X, op=ALU.add)
                    return
                for j in range(NT // 2, NT):
                    v.tensor_reduce(out=lnst[:, j, 0:1], in_=xa[:, j, :],
                                    axis=AX.X, op=ALU.add)
                v.wait_ge(s_ls, sb * NT + NT)
                v.drain()
                v.tensor_scalar(out=lnst[:, :, 2:3], in0=lnst[:, :, 0:1],
                                scalar1=Cinv, scalar2=None, op0=ALU.mult)
                v.tensor_scalar(out=lnst[:, :, 3:4], in0=lnst[:, :, 1:2],
                                scalar1=Cinv, scalar2=None, op0=ALU.mult)
                v.drain()
                v.tensor_tensor(out=lnst[:, :, 4:5], in0=lnst[:, :, 2:3],
                                in1=lnst[:, :, 2:3], op=ALU.mult)
                v.drain()
                v.tensor_tensor(out=lnst[:, :, 3:4], in0=lnst[:, :, 3:4],
                                in1=lnst[:, :, 4:5],
                                op=ALU.subtract).then_inc(s_lv, 1)

            def st_nmr(sb):
                v.wait_ge(s_lr, sb + 1)
                v.drain()
                v.tensor_tensor(out=lnst[:, :, 4:5], in0=lnst[:, :, 2:3],
                                in1=lnst[:, :, 3:4], op=ALU.mult)
                v.drain()
                v.tensor_scalar(out=lnst[:, :, 4:5], in0=lnst[:, :, 4:5],
                                scalar1=-1.0, scalar2=None,
                                op0=ALU.mult).then_inc(s_lm, 1)

            def st_evac(sb):
                par = sb % 2
                if sb > 1:
                    v.wait_ge(s_o2m, (sb - 1) * NQ * SBB * TPB * ND)
                for blk in range(SBB):
                    for tt in range(TPB):
                        i = (sb * SBB + blk) * TPB + tt
                        v.wait_ge(s_tp, i + 1)
                        v.tensor_copy(
                            out=ln_chan[:, par, blk, :,
                                        tt * 128:(tt + 1) * 128],
                            in_=tp_ps[:, :].rearrange("p (cc t) -> p cc t",
                                                      t=128),
                        ).then_inc(s_ln, 1)

            def pre_gatev(sb):
                par = sb % 2
                for blk in range(SBB):
                    gb = sb * SBB + blk
                    v.wait_ge(s_ge, gb + 1)
                    v.drain()
                    v.tensor_scalar(out=gateF[:, :], in0=gateF[:, :],
                                    scalar1=1.0, scalar2=None, op0=ALU.add)
                    v.drain()
                    v.reciprocal(out=gateF[:, :], in_=gateF[:, :])
                    v.drain()
                    v.tensor_scalar(out=maskF[:, :], in0=gateF[:, :],
                                    scalar1=0.5, scalar2=None, op0=ALU.is_gt)
                    v.drain()
                    v.scalar_tensor_tensor(out=gateF[:, :], in0=gateF[:, :],
                                           scalar=0.5, in1=maskF[:, :],
                                           op0=ALU.mult, op1=ALU.mult)
                    v.drain()
                    if sb > 0:
                        v.wait_ge(s_bm, (gb - SBB + 1) * TPB)
                    v.tensor_copy(out=w_ebf[:, blk, :],
                                  in_=gateF[:, :]).then_inc(s_gw, 1)
                    for tt in range(TPB):
                        i = gb * TPB + tt
                        v.wait_ge(s_wt, i + 1)
                        v.tensor_copy(out=w_tok[:, par, blk, tt, :],
                                      in_=bias_ps[:, 0:E]).then_inc(s_wv, 1)
                        v.wait_ge(s_bm, i + 1)
                        v.drain()
                        v.tensor_copy(out=comb[:, blk, tt, :],
                                      in_=bias_ps[:, :]).then_inc(s_bi, 1)


            for sb in range(NSB):
                par = sb % 2
                if sb == 0:
                    st_vstat(0, 0)
                    st_vstat(0, 1)
                    st_nmr(0)
                    st_evac(0)
                if sb > 0:
                    v.drain()
                pre_gatev(sb)
                for q in range(NQ):
                    e = q // QPE
                    for blk in range(SBB):
                        r = (sb * NQ + q) * SBB + blk
                        if blk == 0:
                            v.drain()
                        for tt in range(TPB):
                            for dh in range(ND):
                                gi = r * TPB * ND + tt * ND + dh
                                v.wait_ge(s_o2m, gi + 1)
                                v.scalar_tensor_tensor(
                                    out=comb[:, blk, tt,
                                             dh * DN:(dh + 1) * DN],
                                    in0=o2_ps[gi % 2][:, :],
                                    scalar=w_tok[:, par, blk, tt, e:e + 1],
                                    in1=comb[:, blk, tt,
                                             dh * DN:(dh + 1) * DN],
                                    op0=ALU.mult, op1=ALU.add,
                                ).then_inc(s_o2f, 1)
                    if sb + 1 < NSB:
                        if q == max(0, NQ - 6):
                            v.drain()
                            st_vstat(sb + 1, 0)
                        if q == max(0, NQ - 5):
                            v.drain()
                            st_vstat(sb + 1, 1)
                        if q == max(1, NQ - 4):
                            st_nmr(sb + 1)
                        if q == max(2, NQ - 3):
                            st_evac(sb + 1)
                for blk in range(SBB):
                    gb = sb * SBB + blk
                    v.wait_ge(s_xb, 16 * (gb + 1))
                    v.drain()
                    v.tensor_tensor(
                        out=xb[:, :, :].rearrange("p tt c -> p (tt c)"),
                        in0=comb[:, blk, :, :].rearrange("p tt c -> p (tt c)"),
                        in1=xb[:, :, :].rearrange("p tt c -> p (tt c)"),
                        op=ALU.add,
                    ).then_inc(s_xn, 1)

        def body_tensor(t):
            def st_tp(sb):
                for blk in range(SBB):
                    gb = sb * SBB + blk
                    for tt in range(TPB):
                        i = gb * TPB + tt
                        t.wait_ge(s_la, i + 1)
                        if i > 0:
                            t.wait_ge(s_ln, i)
                        for cc in range(CC):
                            mm = t.transpose(tp_ps[:, cc * 128:(cc + 1) * 128],
                                             in_=lntok2[:, i % 2,
                                                        cc * 128:
                                                        (cc + 1) * 128],
                                             identity=P["ident"][:, :])
                        mm.then_inc(s_tp, 1)

            def st_gmm(sb):
                par = sb % 2
                for blk in range(SBB):
                    gb = sb * SBB + blk
                    t.wait_ge(s_ln, (gb + 1) * TPB)
                    if gb > 0:
                        t.wait_ge(s_ge, gb)
                    for cc in range(CC):
                        mm = t.matmul(gate_ps[:, :], lhsT=P["wg_pad"][:, cc, :],
                                      rhs=ln_chan[:, par, blk, cc, :],
                                      start=(cc == 0), stop=(cc == CC - 1))
                    mm.then_inc(s_gm, 1)

            def st_wtr_bias(sb):
                for blk in range(SBB):
                    gb = sb * SBB + blk
                    t.wait_ge(s_gw, gb + 1)
                    for tt in range(TPB):
                        i = gb * TPB + tt
                        if i > 0:
                            t.wait_ge(s_bi, i)
                        t.transpose(bias_ps[:, 0:E],
                                    in_=gateF[0:E, tt * 128:(tt + 1) * 128],
                                    identity=P["identF"][0:E, 0:E]
                                    ).then_inc(s_wt, 1)
                        t.wait_ge(s_wv, i + 1)
                        for dh in range(ND):
                            mm = t.matmul(
                                bias_ps[:, dh * DN:(dh + 1) * DN],
                                lhsT=w_ebf[:, blk, tt * 128:(tt + 1) * 128],
                                rhs=P["b2_bf"][:, dh * DN:(dh + 1) * DN],
                                start=True, stop=True)
                        mm.then_inc(s_bm, 1)

            for sb in range(NSB):
                par = sb % 2
                if sb == 0:
                    st_tp(0)
                    st_gmm(0)
                st_wtr_bias(sb)
                for q in range(NQ):
                    L = sb * NQ + q
                    slot = L % 2
                    t.wait_ge(s_w[slot], 32 * (L // 2 + 1))
                    for blk in range(SBB):
                        r = (sb * NQ + q) * SBB + blk
                        for hc in range(HCQ):
                            gi = r * HCQ + hc
                            if gi >= 2:
                                t.wait_ge(s_hev, gi - 1)
                            for cc in range(CC):
                                mm = t.matmul(
                                    h_ps[gi % 2][:, :],
                                    lhsT=w1q[:, slot, cc,
                                             hc * 128:(hc + 1) * 128],
                                    rhs=ln_chan[:, par, blk, cc, :],
                                    start=(cc == 0), stop=(cc == CC - 1))
                            mm.then_inc(s_hmm, 1)
                        t.wait_ge(s_hev, (r + 1) * HCQ)
                        for tt in range(TPB):
                            for dh in range(ND):
                                gi = r * TPB * ND + tt * ND + dh
                                if gi >= 2:
                                    t.wait_ge(s_o2f, gi - 1)
                                for hc in range(HCQ):
                                    mm = t.matmul(
                                        o2_ps[gi % 2][:, :],
                                        lhsT=h_sb[:, hc,
                                                  tt * 128:(tt + 1) * 128],
                                        rhs=w2q[:, slot, hc,
                                                dh * DN:(dh + 1) * DN],
                                        start=(hc == 0),
                                        stop=(hc == HCQ - 1))
                                mm.then_inc(s_o2m, 1)
                    if sb + 1 < NSB:
                        if q == max(2, NQ - 3):
                            st_tp(sb + 1)
                        if q == NQ - 1:
                            st_gmm(sb + 1)

        block = E_(nc.Block(label))
        block.sync(body_sync)
        block.scalar(body_scalar)
        block.vector(body_vector)
        block.tensor(body_tensor)


def emit_attn(nc, cfg, P, WT4, x_in, x_out, label="at"):
    C, S, T, TC = cfg.C, cfg.S, cfg.T, cfg.TC
    CC, G, NCH, ND, DN = cfg.CC, cfg.G, cfg.NCH, cfg.ND, cfg.DN
    Cinv = 1.0 / C

    with ExitStack() as ctx:
        E_ = ctx.enter_context
        xat = E_(nc.sbuf_tensor(f"{label}_xat", [128, 2, S, C], F32))
        UREP = P["urep"]
        WREP = P["wrep"]
        ln2 = E_(nc.sbuf_tensor(f"{label}_ln2", [128, G, TC + 1], BF16))
        xmx = E_(nc.sbuf_tensor(f"{label}_xmx", [128, 3, G, TC], BF16))
        dmx = E_(nc.sbuf_tensor(f"{label}_dmx", [128, G, TC], BF16))
        kb = E_(nc.sbuf_tensor(f"{label}_kb", [128, G, TC], F32))
        vb = E_(nc.sbuf_tensor(f"{label}_vb", [128, G, TC], F32))
        sg = E_(nc.sbuf_tensor(f"{label}_sg", [128, G, TC], F32))
        ku = E_(nc.sbuf_tensor(f"{label}_ku", [128, G, TC], F32))
        sc1 = E_(nc.sbuf_tensor(f"{label}_sc1", [128, G, TC], F32))
        sc2 = E_(nc.sbuf_tensor(f"{label}_sc2", [128, G, TC], F32))
        sc3 = E_(nc.sbuf_tensor(f"{label}_sc3", [128, G, TC], F32))
        ob = E_(nc.sbuf_tensor(f"{label}_ob", [128, G, TC + 1], F32))
        hb = E_(nc.sbuf_tensor(f"{label}_hb", [128, G, TC + 1], F32))
        qb = E_(nc.sbuf_tensor(f"{label}_qb", [128, G, TC + 1], F32))
        rwk = E_(nc.sbuf_tensor(f"{label}_rwk", [128, G, TC], BF16))
        lntok2 = E_(nc.sbuf_tensor(f"{label}_lntok", [128, 2, C], BF16))
        lnst = E_(nc.sbuf_tensor(f"{label}_lnst", [128, S, 5], F32))

        kv_ps = [E_(nc.psum_tensor(f"{label}_kvps{i}", [128, 4 * TC], F32))
                 for i in range(2)]
        tp_ps = E_(nc.psum_tensor(f"{label}_tpps", [128, C], BF16))
        wo_ps = [E_(nc.psum_tensor(f"{label}_wops{i}", [128, DN], F32))
                 for i in range(2)]

        s_ax = [E_(nc.semaphore(f"{label}_sax{i}")) for i in range(2)]
        s_ls = E_(nc.semaphore(f"{label}_sls"))
        s_lv = E_(nc.semaphore(f"{label}_slv"))
        s_lr = E_(nc.semaphore(f"{label}_slr"))
        s_lm = E_(nc.semaphore(f"{label}_slm"))
        s_la = E_(nc.semaphore(f"{label}_sla"))
        s_tp = E_(nc.semaphore(f"{label}_stp"))
        s_ln = E_(nc.semaphore(f"{label}_sln"))
        s_mx = E_(nc.semaphore(f"{label}_smx"))
        s_km = E_(nc.semaphore(f"{label}_skm"))
        s_rev = E_(nc.semaphore(f"{label}_srev"))
        s_vs = E_(nc.semaphore(f"{label}_svs"))
        s_ss = E_(nc.semaphore(f"{label}_sss"))
        s_sE = E_(nc.semaphore(f"{label}_ssE"))
        s_sA = E_(nc.semaphore(f"{label}_ssA"))
        s_rw = E_(nc.semaphore(f"{label}_srw"))
        s_vz = E_(nc.semaphore(f"{label}_svz"))
        s_sgf = E_(nc.semaphore(f"{label}_ssgf"))
        s_wm = E_(nc.semaphore(f"{label}_swm"))
        s_x2 = E_(nc.semaphore(f"{label}_sx2"))
        s_xo = E_(nc.semaphore(f"{label}_sxo"))

        WTk, WTv, WTr, WTo = WT4
        WT = {"k": WTk, "v": WTv, "r": WTr}
        TM = {"k": P["tmk_cols"], "v": P["tmv_cols"], "r": P["tmr_cols"]}
        NB4 = 3 * G // 4

        def body_sync(sync):
            def load(cb):
                for si in range(S):
                    sync.dma_start(
                        out=xat[:, cb % 2, si, :],
                        in_=x_in[si * T + cb * TC: si * T + (cb + 1) * TC, :],
                    ).then_inc(s_ax[cb % 2], 16)
            load(0)
            if NCH > 1:
                load(1)
            for cb in range(NCH):
                for si in range(S):
                    sync.wait_ge(s_x2, cb * S + si + 1)
                    sync.dma_start(
                        out=x_out[si * T + cb * TC: si * T + (cb + 1) * TC, :],
                        in_=xat[:, cb % 2, si, :],
                    ).then_inc(s_xo, 16)
                if cb + 2 < NCH:
                    sync.wait_ge(s_xo, 16 * S * (cb + 1))
                    load(cb + 2)
            sync.wait_ge(s_xo, 16 * S * NCH)

        def body_scalar(s):
            def sfrontA(cb):
                s.wait_ge(s_ax[cb % 2], 16 * S * (cb // 2 + 1))
                s.wait_ge(s_mx, cb * 3)
                sqscr = dmx[:, 0:C // TC, :].rearrange("p g t -> p (g t)")
                _ln_blk_scalar(s, P, [xat[:, cb % 2, si, :] for si in range(S)],
                               cb, S, sqscr, lnst, lntok2,
                               s_ls, s_lv, s_lr, s_lm, s_la, s_tp)
                s.drain()
                if cb == 0:
                    s.wait_ge(s_vz, 1)
                else:
                    s.activation(out=ln2[:, :, 0:1], in_=ln2[:, :, TC:TC + 1],
                                 func=AF.Identity, bias=0.0, scale=1.0)
                for si in range(S):
                    s.wait_ge(s_tp, cb * S + si + 1)
                    s.drain()
                    s.activation(
                        out=ln2[:, si * CC:(si + 1) * CC, 1:TC + 1],
                        in_=tp_ps[:, :].rearrange("p (cc t) -> p cc t", t=TC),
                        func=AF.Identity, bias=0.0,
                        scale=1.0).then_inc(s_ln, 1)

            def evac_batches(cb, lo, hi, guard=None):
                base = cb * 3 * G
                for b in range(lo, hi):
                    gi_last = base + b * 4 + 3
                    s.wait_ge(s_km, gi_last + 1)
                    if guard is not None:
                        s.wait_ge(*guard)
                        guard_used = True
                    bank = ((base + b * 4) // 4) % 2
                    q_of = (b * 4) // G
                    g0 = (b * 4) % G
                    if b == lo:
                        s.drain()
                    if q_of < 2:
                        buf = kb if q_of == 0 else vb
                        s.activation(out=buf[:, g0:g0 + 4, :],
                                     in_=kv_ps[bank][:, :], func=AF.Identity,
                                     bias=0.0, scale=1.0).then_inc(s_rev, 1)
                    else:
                        s.activation(out=sg[:, g0:g0 + 4, :],
                                     in_=kv_ps[bank][:, :], func=AF.Exp,
                                     bias=0.0, scale=-1.0).then_inc(s_rev, 1)

            def sigmoid_fin(cb):
                s.drain()
                s.activation(out=sg[:, :, :], in_=sg[:, :, :], func=AF.Ln,
                             bias=1.0, scale=1.0)
                s.drain()
                s.activation(out=sg[:, :, :], in_=sg[:, :, :], func=AF.Exp,
                             bias=0.0, scale=-1.0).then_inc(s_sgf, 1)

            def sback(cb, nxt):
                s.wait_ge(s_vs, cb * 3 + 1)
                s.drain()
                s.activation(out=sc1[:, :, :], in_=sc1[:, :, :], func=AF.Exp,
                             bias=0.0, scale=1.0).then_inc(s_sE, 1)
                s.activation(out=sc2[:, :, :], in_=sc2[:, :, :], func=AF.Exp,
                             bias=0.0, scale=1.0).then_inc(s_ss, 1)
                if nxt is not None:
                    evac_batches(nxt, 0, G // 4)
                s.wait_ge(s_vs, cb * 3 + 2)
                s.drain()
                s.activation(out=sc3[:, :, :], in_=sc3[:, :, :], func=AF.Exp,
                             bias=0.0, scale=1.0).then_inc(s_sA, 1)
                s.activation(out=ku[:, :, :], in_=ku[:, :, :], func=AF.Exp,
                             bias=0.0, scale=1.0).then_inc(s_ss, 1)
                s.wait_ge(s_vs, cb * 3 + 3)
                s.drain()
                s.activation(out=sc3[:, :, :], in_=sc3[:, :, :], func=AF.Ln,
                             bias=0.0, scale=1.0)
                s.drain()
                s.activation(out=sc3[:, :, :], in_=sc3[:, :, :], func=AF.Exp,
                             bias=0.0, scale=-1.0).then_inc(s_ss, 1)
                if nxt is not None:
                    evac_batches(nxt, G // 4, 2 * G // 4)
                    evac_batches(nxt, 2 * G // 4, NB4,
                                 guard=(s_rw, cb + 1))
                    sigmoid_fin(nxt)

            sfrontA(0)
            evac_batches(0, 0, NB4)
            sigmoid_fin(0)
            for cb in range(NCH):
                if cb + 1 < NCH:
                    sfrontA(cb + 1)
                sback(cb, cb + 1 if cb + 1 < NCH else None)

        def body_vector(v):
            def vfront(cb):
                par = cb % 2
                if cb == 0:
                    v.memset(ln2[:, :, 0:1], 0.0).then_inc(s_vz, 1)
                v.wait_ge(s_ax[cb % 2], 16 * S * (cb // 2 + 1))
                if cb > 0:
                    v.wait_ge(s_la, cb * S)
                _ln_blk_vector(v, Cinv,
                               [xat[:, cb % 2, si, :] for si in range(S)],
                               cb, S, lnst, s_ls, s_lv, s_lr, s_lm)
                v.wait_ge(s_ln, (cb + 1) * S)
                v.tensor_tensor(out=dmx[:, :, :], in0=ln2[:, :, 1:TC + 1],
                                in1=ln2[:, :, 0:TC], op=ALU.subtract)
                v.drain()
                for iq, qn in enumerate(("k", "v", "r")):
                    qidx = cb * 3 + iq
                    need = (qidx - 2) * G
                    if need > 0:
                        v.wait_ge(s_km, need)
                    last = None
                    for g in range(G):
                        cc = g % CC
                        last = v.scalar_tensor_tensor(
                            out=xmx[:, qidx % 3, g, :], in0=dmx[:, g, :],
                            scalar=TM[qn][:, cc:cc + 1],
                            in1=ln2[:, g, 0:TC],
                            op0=ALU.mult, op1=ALU.add)
                    last.then_inc(s_mx, 1)

            def vback(cb):
                v.wait_ge(s_rev, cb * NB4 + G // 4)
                v.drain()
                if cb == 0:
                    v.memset(ob[:, :, 0:1], NEG)
                    v.memset(hb[:, :, 0:1], 0.0)
                    v.memset(qb[:, :, 0:1], 0.0)
                else:
                    v.tensor_copy(out=ob[:, :, 0:1], in_=ob[:, :, TC:TC + 1])
                    v.tensor_copy(out=hb[:, :, 0:1], in_=hb[:, :, TC:TC + 1])
                    v.tensor_copy(out=qb[:, :, 0:1], in_=qb[:, :, TC:TC + 1])
                v.tensor_tensor(out=ku[:, :, :], in0=kb[:, :, :],
                                in1=UREP[:, :, :], op=ALU.add)
                v.drain()
                for g in range(G):
                    v.tensor_tensor_scan(out=ob[:, g, 1:TC + 1],
                                         data0=WREP[:, g, :],
                                         data1=kb[:, g, :],
                                         initial=ob[:, g, 0:1],
                                         op0=ALU.add, op1=ALU.max)
                v.drain()
                v.tensor_tensor(out=sc1[:, :, :], in0=kb[:, :, :],
                                in1=ob[:, :, 1:TC + 1], op=ALU.subtract)
                v.tensor_tensor(out=sc2[:, :, :], in0=ob[:, :, 0:TC],
                                in1=ob[:, :, 1:TC + 1], op=ALU.subtract)
                v.drain()
                v.tensor_tensor(out=sc2[:, :, :], in0=sc2[:, :, :],
                                in1=WREP[:, :, :],
                                op=ALU.add).then_inc(s_vs, 1)
                v.wait_ge(s_sE, cb + 1)
                v.wait_ge(s_rev, cb * NB4 + 2 * (G // 4))
                v.tensor_tensor(out=sc3[:, :, :], in0=sc1[:, :, :],
                                in1=vb[:, :, :], op=ALU.mult)
                v.wait_ge(s_ss, cb * 3 + 1)
                v.drain()
                for g in range(G):
                    v.tensor_tensor_scan(out=hb[:, g, 1:TC + 1],
                                         data0=sc2[:, g, :],
                                         data1=sc3[:, g, :],
                                         initial=hb[:, g, 0:1],
                                         op0=ALU.mult, op1=ALU.add)
                for g in range(G):
                    v.tensor_tensor_scan(out=qb[:, g, 1:TC + 1],
                                         data0=sc2[:, g, :],
                                         data1=sc1[:, g, :],
                                         initial=qb[:, g, 0:1],
                                         op0=ALU.mult, op1=ALU.add)
                v.drain()
                v.tensor_tensor(out=sc1[:, :, :], in0=ob[:, :, 0:TC],
                                in1=ku[:, :, :], op=ALU.max)
                v.drain()
                v.tensor_tensor(out=sc3[:, :, :], in0=ob[:, :, 0:TC],
                                in1=sc1[:, :, :], op=ALU.subtract)
                v.tensor_tensor(out=ku[:, :, :], in0=ku[:, :, :],
                                in1=sc1[:, :, :],
                                op=ALU.subtract).then_inc(s_vs, 1)
                v.wait_ge(s_sA, cb + 1)
                v.tensor_tensor(out=sc2[:, :, :], in0=sc3[:, :, :],
                                in1=hb[:, :, 0:TC], op=ALU.mult)
                v.wait_ge(s_ss, cb * 3 + 2)
                v.tensor_tensor(out=vb[:, :, :], in0=ku[:, :, :],
                                in1=vb[:, :, :], op=ALU.mult)
                v.tensor_tensor(out=sc3[:, :, :], in0=sc3[:, :, :],
                                in1=qb[:, :, 0:TC], op=ALU.mult)
                v.drain()
                v.tensor_tensor(out=sc2[:, :, :], in0=sc2[:, :, :],
                                in1=vb[:, :, :], op=ALU.add)
                v.tensor_tensor(out=sc3[:, :, :], in0=sc3[:, :, :],
                                in1=ku[:, :, :],
                                op=ALU.add).then_inc(s_vs, 1)
                v.wait_ge(s_ss, cb * 3 + 3)
                v.drain()
                v.tensor_tensor(out=sc2[:, :, :], in0=sc2[:, :, :],
                                in1=sc3[:, :, :], op=ALU.mult)
                v.wait_ge(s_sgf, cb + 1)
                v.drain()
                v.tensor_tensor(out=rwk[:, :, :], in0=sc2[:, :, :],
                                in1=sg[:, :, :], op=ALU.mult).then_inc(s_rw, 1)
                for si in range(S):
                    last = None
                    for dh in range(ND):
                        gi = cb * S * ND + si * ND + dh
                        v.wait_ge(s_wm, gi + 1)
                        last = v.tensor_tensor(
                            out=xat[:, cb % 2, si, dh * DN:(dh + 1) * DN],
                            in0=wo_ps[gi % 2][:, :],
                            in1=xat[:, cb % 2, si, dh * DN:(dh + 1) * DN],
                            op=ALU.add)
                    last.then_inc(s_x2, 1)

            vfront(0)
            for cb in range(NCH):
                if cb + 1 < NCH:
                    vfront(cb + 1)
                vback(cb)

        def body_tensor(t):
            def tfront(cb):
                par = cb % 2
                for si in range(S):
                    i = cb * S + si
                    t.wait_ge(s_la, i + 1)
                    if i > 0:
                        t.wait_ge(s_ln, i)
                    for cc in range(CC):
                        mm = t.transpose(tp_ps[:, cc * TC:(cc + 1) * TC],
                                         in_=lntok2[:, i % 2,
                                                    cc * 128:(cc + 1) * 128],
                                         identity=P["ident"][:, :])
                    mm.then_inc(s_tp, 1)
                base = cb * 3 * G
                for iq, qn in enumerate(("k", "v", "r")):
                    qidx = cb * 3 + iq
                    t.wait_ge(s_mx, qidx + 1)
                    for si in range(S):
                        for dc in range(CC):
                            gi = base + iq * G + si * CC + dc
                            bank, slot = (gi // 4) % 2, gi % 4
                            if gi >= 8:
                                t.wait_ge(s_rev, (gi - 8) // 4 + 1)
                            for cc in range(CC):
                                mm = t.matmul(
                                    kv_ps[bank][:, slot * TC:(slot + 1) * TC],
                                    lhsT=WT[qn][:, cc,
                                                dc * 128:(dc + 1) * 128],
                                    rhs=xmx[:, qidx % 3, si * CC + cc, :],
                                    start=(cc == 0), stop=(cc == CC - 1))
                            mm.then_inc(s_km, 1)

            def tback(cb):
                t.wait_ge(s_rw, cb + 1)
                for si in range(S):
                    for dh in range(ND):
                        gi = cb * S * ND + si * ND + dh
                        if gi >= 2:
                            prev_si = ((gi - 2) % (S * ND)) // ND
                            prev_cb = (gi - 2) // (S * ND)
                            t.wait_ge(s_x2, prev_cb * S + prev_si + 1)
                        for cc in range(CC):
                            mm = t.matmul(
                                wo_ps[gi % 2][:, :],
                                lhsT=rwk[:, si * CC + cc, :],
                                rhs=WTo[:, cc, dh * DN:(dh + 1) * DN],
                                start=(cc == 0), stop=(cc == CC - 1))
                        mm.then_inc(s_wm, 1)

            tfront(0)
            for cb in range(NCH):
                if cb + 1 < NCH:
                    tfront(cb + 1)
                tback(cb)

        block = E_(nc.Block(label))
        block.sync(body_sync)
        block.scalar(body_scalar)
        block.vector(body_vector)
        block.tensor(body_tensor)


def emit_prologue(nc, cfg, P, ins, w1c, w2c, s_gx, s_gE):
    C, E, H, CC, TC = cfg.C, cfg.E, cfg.H, cfg.CC, cfg.TC
    HCALL = cfg.HCALL

    with ExitStack() as ctx:
        E_ = ctx.enter_context
        b1raw = E_(nc.sbuf_tensor("p0_b1raw", [E, H], F32))
        b2raw = E_(nc.sbuf_tensor("p0_b2raw", [E, C], F32))
        smallT = E_(nc.sbuf_tensor("p0_small", [CC, 128 * 5], F32))
        pp = E_(nc.psum_tensor("p0_pp", [128, 128], F32))
        s_d = E_(nc.semaphore("p0_sd"))
        s_g = s_gx
        s_t = E_(nc.semaphore("p0_st"))
        s_v = E_(nc.semaphore("p0_sv"))
        s_s = E_(nc.semaphore("p0_ss"))

        n_tp = HCALL + 5
        small_names = ["td", "tf", "tmk", "tmv", "tmr"]
        col_targets = [P["w_cols"], P["u_cols"], P["tmk_cols"], P["tmv_cols"],
                       P["tmr_cols"]]

        def body_sync(sync):
            sync.dma_start(out=P["ident"][:, :],
                           in_=ins["ident"][:, :]).then_inc(s_d, 16)
            sync.dma_start(out=P["identF"][:, :],
                           in_=ins["identF"][:, :]).then_inc(s_d, 16)
            sync.dma_start(out=b1raw[:, :], in_=ins["b1"][:, :]).then_inc(s_d, 16)
            sync.dma_start(out=b2raw[:, :], in_=ins["b2"][:, :]).then_inc(s_d, 16)
            sync.dma_start(out=P["negbg_col"][0:E, :],
                           in_=ins["bg"][:].rearrange("(e u) -> e u", u=1)
                           ).then_inc(s_d, 16)
            for j, nm in enumerate(small_names):
                sync.dma_start(out=smallT[:, j * 128:(j + 1) * 128],
                               in_=ins[nm][:].rearrange("(cc p) -> cc p", p=128)
                               ).then_inc(s_d, 16)

        def body_gpsimd(g):
            g.wait_ge(s_v, 1)
            g.dma_start(out=P["wg_pad"][:, :, 0:E],
                        in_=ins["wg"][:, :].rearrange("(cc p) e -> p cc e",
                                                      p=128)).then_inc(s_g, 16)
            for e in range(E):
                g.dma_start(out=w1c[e, :, :], in_=ins["w1"][e, :, :],
                            max_dma_last_dim=8192).then_inc(s_gE[e], 16)
                g.dma_start(out=w2c[e, :, :], in_=ins["w2"][e, :, :],
                            max_dma_last_dim=8192).then_inc(s_gE[e], 16)
            g.wait_ge(s_g, 16)

        def body_tensor(t):
            t.wait_ge(s_d, 16 * (5 + len(small_names)))
            for hq in range(HCALL):
                if hq > 0:
                    t.wait_ge(s_v, 2 + hq)
                t.transpose(pp[:, 0:E], in_=b1raw[:, hq * 128:(hq + 1) * 128],
                            identity=P["identF"][0:E, 0:E]).then_inc(s_t, 1)
            for j in range(5):
                t.wait_ge(s_v, 2 + HCALL + j)
                t.transpose(pp[:, 0:CC], in_=smallT[:, j * 128:(j + 1) * 128],
                            identity=P["identF"][0:CC, 0:CC]).then_inc(s_t, 1)

        def body_vector(v):
            v.memset(P["wg_pad"][:, :, :], 0.0)
            v.drain()
            v.memset(P["eps_col"][:, :], 1e-5).then_inc(s_v, 1)
            v.wait_ge(s_d, 16 * (5 + len(small_names)))
            v.tensor_scalar(out=P["negbg_col"][0:E, :],
                            in0=P["negbg_col"][0:E, :], scalar1=-1.0,
                            scalar2=None, op0=ALU.mult)
            v.drain()
            v.tensor_copy(out=P["b2_bf"][:, :], in_=b2raw[:, :]).then_inc(s_v, 1)
            for hq in range(HCALL):
                v.wait_ge(s_t, hq + 1)
                v.tensor_copy(
                    out=P["b1_cols"][:, :, hq:hq + 1]
                    .rearrange("p e u -> p (e u)"),
                    in_=pp[:, 0:E]).then_inc(s_v, 1)
            for j in range(5):
                v.wait_ge(s_t, HCALL + j + 1)
                v.tensor_copy(out=col_targets[j][:, :],
                              in_=pp[:, 0:CC]).then_inc(s_v, 1)
            v.wait_ge(s_s, 1)
            v.drain()
            v.tensor_scalar(out=P["w_cols"][:, :], in0=P["w_cols"][:, :],
                            scalar1=-1.0, scalar2=None, op0=ALU.mult)
            v.drain()

        def body_scalar(s):
            s.wait_ge(s_v, 2 + HCALL + 1)
            s.drain()
            s.activation(out=P["w_cols"][:, :], in_=P["w_cols"][:, :],
                         func=AF.Exp, bias=0.0, scale=1.0).then_inc(s_s, 1)

        block = E_(nc.Block("p0", no_gpsimd_drain=True))
        block.sync(body_sync)
        block.gpsimd(body_gpsimd)
        block.tensor(body_tensor)
        block.vector(body_vector)
        block.scalar(body_scalar)


def emit_wtrans(nc, cfg, P, ins, WT4):
    C, CC = cfg.C, cfg.CC
    names = ["wk", "wv", "wr", "wo"]
    with ExitStack() as ctx:
        E_ = ctx.enter_context
        wtmp = E_(nc.sbuf_tensor("pw_wtmp", [128, C], BF16))
        pw_ps = E_(nc.psum_tensor("pw_ps", [128, C], BF16))
        s_g = E_(nc.semaphore("pw_sg"))
        s_t = E_(nc.semaphore("pw_st"))
        s_v = E_(nc.semaphore("pw_sv"))

        def body_gpsimd(g):
            nd = 0
            for wi in range(4):
                for dc in range(CC):
                    if nd > 0:
                        g.wait_ge(s_t, nd)
                    g.dma_start(out=wtmp[:, :],
                                in_=ins[names[wi]][dc * 128:(dc + 1) * 128, :]
                                ).then_inc(s_g, 16)
                    nd += 1

        def body_tensor(t):
            nd = 0
            for wi in range(4):
                for dc in range(CC):
                    t.wait_ge(s_g, 16 * (nd + 1))
                    if nd > 0:
                        t.wait_ge(s_v, nd)
                    for cc in range(CC):
                        mm = t.transpose(pw_ps[:, cc * 128:(cc + 1) * 128],
                                         in_=wtmp[:, cc * 128:(cc + 1) * 128],
                                         identity=P["ident"][:, :])
                    mm.then_inc(s_t, 1)
                    nd += 1

        def body_vector(v):
            G = cfg.G
            v.memset(P["urep"][:, :, :], 0.0)
            v.drain()
            v.memset(P["wrep"][:, :, :], 0.0)
            v.drain()
            for g in range(G):
                v.tensor_scalar(out=P["urep"][:, g, :], in0=P["urep"][:, g, :],
                                scalar1=P["u_cols"][:, g % CC:g % CC + 1],
                                scalar2=None, op0=ALU.add)
            for g in range(G):
                v.tensor_scalar(out=P["wrep"][:, g, :], in0=P["wrep"][:, g, :],
                                scalar1=P["w_cols"][:, g % CC:g % CC + 1],
                                scalar2=None, op0=ALU.add)
            v.drain()
            nd = 0
            for wi in range(4):
                for dc in range(CC):
                    v.wait_ge(s_t, nd + 1)
                    v.tensor_copy(
                        out=WT4[wi][:, :, dc * 128:(dc + 1) * 128],
                        in_=pw_ps[:, :].rearrange("p (cc d) -> p cc d", d=128),
                    ).then_inc(s_v, 1)
                    nd += 1

        block = E_(nc.Block("pw"))
        block.gpsimd(body_gpsimd)
        block.tensor(body_tensor)
        block.vector(body_vector)


def build_program(cfg):
    nc = bass.Bass("TRN2", target_bir_lowering=False)
    C, E, H, TOK = cfg.C, cfg.E, cfg.H, cfg.TOK
    ins = {
        "x": nc.dram_tensor("x", [TOK, C], F32, kind="ExternalInput").ap(),
        "w1": nc.dram_tensor("w1", [E, C, H], F32, kind="ExternalInput").ap(),
        "w2": nc.dram_tensor("w2", [E, H, C], F32, kind="ExternalInput").ap(),
        "wk": nc.dram_tensor("wk", [C, C], F32, kind="ExternalInput").ap(),
        "wv": nc.dram_tensor("wv", [C, C], F32, kind="ExternalInput").ap(),
        "wr": nc.dram_tensor("wr", [C, C], F32, kind="ExternalInput").ap(),
        "wo": nc.dram_tensor("wo", [C, C], F32, kind="ExternalInput").ap(),
        "wg": nc.dram_tensor("wg", [C, E], F32, kind="ExternalInput").ap(),
        "bg": nc.dram_tensor("bg", [E], F32, kind="ExternalInput").ap(),
        "b1": nc.dram_tensor("b1", [E, H], F32, kind="ExternalInput").ap(),
        "b2": nc.dram_tensor("b2", [E, C], F32, kind="ExternalInput").ap(),
        "td": nc.dram_tensor("td", [C], F32, kind="ExternalInput").ap(),
        "tf": nc.dram_tensor("tf", [C], F32, kind="ExternalInput").ap(),
        "tmk": nc.dram_tensor("tmk", [C], F32, kind="ExternalInput").ap(),
        "tmv": nc.dram_tensor("tmv", [C], F32, kind="ExternalInput").ap(),
        "tmr": nc.dram_tensor("tmr", [C], F32, kind="ExternalInput").ap(),
        "ident": nc.dram_tensor("ident", [128, 128], BF16,
                                kind="ExternalInput").ap(),
        "identF": nc.dram_tensor("identF", [128, 128], F32,
                                 kind="ExternalInput").ap(),
    }
    out = nc.dram_tensor("out", [TOK, C], F32, kind="ExternalOutput").ap()
    x1 = nc.dram_tensor("x1s", [TOK, C], F32, kind="Internal").ap()
    x2 = nc.dram_tensor("x2s", [TOK, C], F32, kind="Internal").ap()
    w1c = nc.dram_tensor("w1c", [E, C, H], BF16, kind="Internal").ap()
    w2c = nc.dram_tensor("w2c", [E, H, C], BF16, kind="Internal").ap()

    with ExitStack() as big:
        B_ = big.enter_context
        s_cache = B_(nc.semaphore("s_cache"))
        s_cacheE = [B_(nc.semaphore(f"s_cachee{e}")) for e in range(E)]
        P = {
            "ident": B_(nc.sbuf_tensor("P_ident", [128, 128], BF16)),
            "identF": B_(nc.sbuf_tensor("P_identF", [128, 128], F32)),
            "wg_pad": B_(nc.sbuf_tensor("P_wgpad", [128, cfg.CC, 128], BF16)),
            "negbg_col": B_(nc.sbuf_tensor("P_negbg", [128, 1], F32)),
            "b1_cols": B_(nc.sbuf_tensor("P_b1c", [128, E, cfg.HCALL], F32)),
            "b2_bf": B_(nc.sbuf_tensor("P_b2bf", [E, C], BF16)),
            "eps_col": B_(nc.sbuf_tensor("P_eps", [128, 1], F32)),
            "w_cols": B_(nc.sbuf_tensor("P_wcol", [128, cfg.CC], F32)),
            "u_cols": B_(nc.sbuf_tensor("P_ucol", [128, cfg.CC], F32)),
            "tmk_cols": B_(nc.sbuf_tensor("P_tmk", [128, cfg.CC], F32)),
            "tmv_cols": B_(nc.sbuf_tensor("P_tmv", [128, cfg.CC], F32)),
            "tmr_cols": B_(nc.sbuf_tensor("P_tmr", [128, cfg.CC], F32)),
        }
        emit_prologue(nc, cfg, P, ins, w1c, w2c, s_cache, s_cacheE)
        emit_ffn(nc, cfg, P, ins["x"], x1, w1c, w2c, label="f1",
                 cache_sem=s_cacheE)
        with ExitStack() as attn_ctx:
            A_ = attn_ctx.enter_context
            WT4 = [A_(nc.sbuf_tensor(f"WT{i}", [128, cfg.CC, C], BF16))
                   for i in range(4)]
            P["urep"] = A_(nc.sbuf_tensor("P_urep", [128, cfg.G, cfg.TC], BF16))
            P["wrep"] = A_(nc.sbuf_tensor("P_wrep", [128, cfg.G, cfg.TC], F32))
            emit_wtrans(nc, cfg, P, ins, WT4)
            emit_attn(nc, cfg, P, WT4, x1, x2, label="at")
        emit_ffn(nc, cfg, P, x2, out, w1c, w2c, label="f2")
    return nc


def kernel(**inputs):
    import ml_dtypes
    cfg = Cfg()
    n_cores = 8
    B = 16
    x = np.ascontiguousarray(inputs["L0_attn_x"], dtype=np.float32)
    shared = {
        "w1": np.ascontiguousarray(inputs["L0_ffn_W1"], dtype=np.float32),
        "w2": np.ascontiguousarray(inputs["L0_ffn_W2"], dtype=np.float32),
        "wk": np.ascontiguousarray(inputs["L0_attn_Wk"], dtype=np.float32),
        "wv": np.ascontiguousarray(inputs["L0_attn_Wv"], dtype=np.float32),
        "wr": np.ascontiguousarray(inputs["L0_attn_Wr"], dtype=np.float32),
        "wo": np.ascontiguousarray(inputs["L0_attn_Wo"], dtype=np.float32),
        "wg": np.ascontiguousarray(inputs["L0_ffn_Wg"], dtype=np.float32),
        "bg": np.ascontiguousarray(inputs["L0_ffn_bg"], dtype=np.float32),
        "b1": np.ascontiguousarray(inputs["L0_ffn_b1"], dtype=np.float32),
        "b2": np.ascontiguousarray(inputs["L0_ffn_b2"], dtype=np.float32),
        "td": np.ascontiguousarray(inputs["L0_attn_time_decay"],
                                   dtype=np.float32),
        "tf": np.ascontiguousarray(inputs["L0_attn_time_first"],
                                   dtype=np.float32),
        "tmk": np.ascontiguousarray(inputs["L0_attn_time_mix_k"],
                                    dtype=np.float32).reshape(-1),
        "tmv": np.ascontiguousarray(inputs["L0_attn_time_mix_v"],
                                    dtype=np.float32).reshape(-1),
        "tmr": np.ascontiguousarray(inputs["L0_attn_time_mix_r"],
                                    dtype=np.float32).reshape(-1),
        "ident": np.eye(128, dtype=ml_dtypes.bfloat16),
        "identF": np.eye(128, dtype=np.float32),
    }
    spc = B // n_cores
    in_maps = []
    for i in range(n_cores):
        m = dict(shared)
        m["x"] = np.ascontiguousarray(
            x[i * spc:(i + 1) * spc].reshape(cfg.TOK, cfg.C))
        in_maps.append(m)
    nc = build_program(cfg)
    res = run_bass_kernel_spmd(nc, in_maps, core_ids=list(range(n_cores)))
    parts = [res.results[i]["out"].reshape(spc, cfg.T, cfg.C)
             for i in range(n_cores)]
    return np.ascontiguousarray(np.concatenate(parts, axis=0),
                                dtype=np.float32)


# version 16
# speedup vs baseline: 1.0989x; 1.0989x over previous
import numpy as np
from contextlib import ExitStack
import concourse.bass as bass
import concourse.mybir as mybir
from concourse.bass_utils import run_bass_kernel_spmd

F32 = mybir.dt.float32
BF16 = mybir.dt.bfloat16
AF = mybir.ActivationFunctionType
ALU = mybir.AluOpType
AX = mybir.AxisListType

NEG = -1e38


class Cfg:
    def __init__(self, C=1024, H=4096, E=8, T=4096, S=2, TBLK=512, SBB=2, TC=128,
                 HQ=1024):
        self.C, self.H, self.E, self.T, self.S = C, H, E, T, S
        self.TBLK, self.SBB, self.TC, self.HQ = TBLK, SBB, TC, HQ
        self.CC = C // 128
        self.TPB = TBLK // 128
        self.HCQ = HQ // 128
        self.QPE = H // HQ
        self.NQ = E * self.QPE
        self.HCALL = H // 128
        self.TOK = S * T
        self.NBLK = self.TOK // TBLK
        self.NSB = self.NBLK // SBB
        self.NCH = T // TC
        self.ND = max(1, C // 512)
        self.DN = min(512, C)
        self.G = S * self.CC
        assert C % 128 == 0 and H % 128 == 0 and TBLK % 128 == 0
        assert self.TOK % TBLK == 0 and self.NBLK % SBB == 0 and T % TC == 0
        assert TC == 128 and HQ % 128 == 0 and H % HQ == 0 and C % self.DN == 0


def _ln_blk_scalar(s, P, x_aps, blk, NT, sq, st, lntok2, s_ls, s_lv, s_lr,
                   s_lm, s_la, s_tp):
    for tt in range(NT):
        i = blk * NT + tt
        if i > 0 or tt > 0:
            s.drain()
        s.activation(out=sq[:, :], in_=x_aps[tt], func=AF.Square, bias=0.0,
                     scale=1.0, accum_out=st[:, tt, 1:2]).then_inc(s_ls, 1)
    s.wait_ge(s_lv, blk + 1)
    s.activation(out=st[:, :, 3:4], in_=st[:, :, 3:4], func=AF.Ln,
                 bias=P["eps_col"][:, :], scale=1.0)
    s.drain()
    s.activation(out=st[:, :, 3:4], in_=st[:, :, 3:4], func=AF.Exp,
                 bias=0.0, scale=-0.5).then_inc(s_lr, 1)
    s.wait_ge(s_lm, blk + 1)
    for tt in range(NT):
        i = blk * NT + tt
        if i >= 2:
            s.wait_ge(s_tp, i - 1)
        s.activation(out=lntok2[:, i % 2, :], in_=x_aps[tt], func=AF.Identity,
                     scale=st[:, tt, 3:4],
                     bias=st[:, tt, 4:5]).then_inc(s_la, 1)


def _ln_blk_vector(v, Cinv, x_aps, blk, NT, st, s_ls, s_lv, s_lr, s_lm):
    v.drain()
    for tt in range(NT):
        v.tensor_reduce(out=st[:, tt, 0:1], in_=x_aps[tt], axis=AX.X,
                        op=ALU.add)
    v.wait_ge(s_ls, blk * NT + NT)
    v.drain()
    v.tensor_scalar(out=st[:, :, 2:3], in0=st[:, :, 0:1], scalar1=Cinv,
                    scalar2=None, op0=ALU.mult)
    v.tensor_scalar(out=st[:, :, 3:4], in0=st[:, :, 1:2], scalar1=Cinv,
                    scalar2=None, op0=ALU.mult)
    v.drain()
    v.tensor_tensor(out=st[:, :, 4:5], in0=st[:, :, 2:3], in1=st[:, :, 2:3],
                    op=ALU.mult)
    v.drain()
    v.tensor_tensor(out=st[:, :, 3:4], in0=st[:, :, 3:4], in1=st[:, :, 4:5],
                    op=ALU.subtract).then_inc(s_lv, 1)
    v.wait_ge(s_lr, blk + 1)
    v.tensor_tensor(out=st[:, :, 4:5], in0=st[:, :, 2:3], in1=st[:, :, 3:4],
                    op=ALU.mult)
    v.drain()
    v.tensor_scalar(out=st[:, :, 4:5], in0=st[:, :, 4:5], scalar1=-1.0,
                    scalar2=None, op0=ALU.mult).then_inc(s_lm, 1)


def emit_ffn(nc, cfg, P, x_in, x_out, w1c, w2c, label, cache_sem=None):
    C, E, H = cfg.C, cfg.E, cfg.H
    CC, TPB, HCQ, NQ, SBB, NSB, TBLK = (cfg.CC, cfg.TPB, cfg.HCQ, cfg.NQ,
                                        cfg.SBB, cfg.NSB, cfg.TBLK)
    ND, DN, HQ, QPE = cfg.ND, cfg.DN, cfg.HQ, cfg.QPE
    Cinv = 1.0 / C
    n_o2_per_q = SBB * TPB * ND

    with ExitStack() as ctx:
        E_ = ctx.enter_context
        ln_chan = E_(nc.sbuf_tensor(f"{label}_lnch",
                                    [128, 2, SBB, CC, TBLK], BF16))
        comb = E_(nc.sbuf_tensor(f"{label}_comb", [128, SBB, TPB, C], F32))
        w1q = E_(nc.sbuf_tensor(f"{label}_w1q", [128, 2, CC, HQ], BF16))
        w2q = E_(nc.sbuf_tensor(f"{label}_w2q", [128, 2, HCQ, C], BF16))
        h_sb = E_(nc.sbuf_tensor(f"{label}_hsb", [128, HCQ, TBLK], BF16))
        xa = E_(nc.sbuf_tensor(f"{label}_xa", [128, SBB * TPB, C], F32))
        xb = E_(nc.sbuf_tensor(f"{label}_xb", [128, TPB, C], F32))
        w_tok = E_(nc.sbuf_tensor(f"{label}_wtok", [128, 2, SBB, TPB, E], F32))
        w_ebf = E_(nc.sbuf_tensor(f"{label}_webf", [E, SBB, TBLK], BF16))
        gateF = E_(nc.sbuf_tensor(f"{label}_gateF", [E, TBLK], F32))
        maskF = E_(nc.sbuf_tensor(f"{label}_maskF", [E, TBLK], F32))
        sqscr = E_(nc.sbuf_tensor(f"{label}_sqscr", [128, C], BF16))
        lntok2 = E_(nc.sbuf_tensor(f"{label}_lntok", [128, 2, C], BF16))
        lnst = E_(nc.sbuf_tensor(f"{label}_lnst", [128, SBB * TPB, 5], F32))

        h_ps = [E_(nc.psum_tensor(f"{label}_hps{i}", [128, TBLK], F32))
                for i in range(2)]

        def hps(gi):
            if gi % 3 == 2:
                return bias_ps[:, 0:TBLK]
            return h_ps[gi % 3][:, :]
        o2_ps = [E_(nc.psum_tensor(f"{label}_o2ps{i}", [128, DN], F32))
                 for i in range(2)]
        gate_ps = E_(nc.psum_tensor(f"{label}_gps", [128, TBLK], F32))
        tp_ps = E_(nc.psum_tensor(f"{label}_tpps", [128, C], BF16))
        bias_ps = E_(nc.psum_tensor(f"{label}_bps", [128, C], F32))

        s_xa = E_(nc.semaphore(f"{label}_sxa"))
        s_xb = E_(nc.semaphore(f"{label}_sxb"))
        s_w = [E_(nc.semaphore(f"{label}_sw{i}")) for i in range(2)]
        s_ls = E_(nc.semaphore(f"{label}_sls"))
        s_lv = E_(nc.semaphore(f"{label}_slv"))
        s_lr = E_(nc.semaphore(f"{label}_slr"))
        s_lm = E_(nc.semaphore(f"{label}_slm"))
        s_la = E_(nc.semaphore(f"{label}_sla"))
        s_tp = E_(nc.semaphore(f"{label}_stp"))
        s_ln = E_(nc.semaphore(f"{label}_sln"))
        s_gm = E_(nc.semaphore(f"{label}_sgm"))
        s_ge = E_(nc.semaphore(f"{label}_sge"))
        s_gw = E_(nc.semaphore(f"{label}_sgw"))
        s_wt = E_(nc.semaphore(f"{label}_swt"))
        s_wv = E_(nc.semaphore(f"{label}_swv"))
        s_bm = E_(nc.semaphore(f"{label}_sbm"))
        s_bi = E_(nc.semaphore(f"{label}_sbi"))
        s_hmm = E_(nc.semaphore(f"{label}_shmm"))
        s_hev = E_(nc.semaphore(f"{label}_shev"))
        s_o2m = E_(nc.semaphore(f"{label}_so2m"))
        s_o2f = E_(nc.semaphore(f"{label}_so2f"))
        s_xn = E_(nc.semaphore(f"{label}_sxn"))
        s_xo = E_(nc.semaphore(f"{label}_sxo"))

        def tok0(sb, blk):
            return (sb * SBB + blk) * TBLK

        def body_sync(sync):
            nxo = 0

            def loads(sb):
                if sb > 0:
                    sync.wait_ge(s_la, sb * SBB * TPB)
                sync.dma_start(
                    out=xa[:, :, :],
                    in_=x_in[tok0(sb, 0):tok0(sb, 0) + SBB * TBLK, :]
                    .rearrange("(tt p) c -> p tt c", p=128),
                ).then_inc(s_xa, 16)

            loads(0)
            for sb in range(NSB):
                for q in range(NQ):
                    L = sb * NQ + q
                    e, qi = q // QPE, q % QPE
                    if cache_sem is not None and sb == 0:
                        sync.wait_ge(cache_sem[e], 32)
                    if L >= 2:
                        sync.wait_ge(s_o2f, (L - 1) * n_o2_per_q)
                    slot = L % 2
                    sync.dma_start(
                        out=w1q[:, slot, :, :],
                        in_=w1c[e, :, qi * HQ:(qi + 1) * HQ]
                        .rearrange("(cc p) h -> p cc h", p=128),
                    ).then_inc(s_w[slot], 16)
                    sync.dma_start(
                        out=w2q[:, slot, :, :],
                        in_=w2c[e, qi * HQ:(qi + 1) * HQ, :]
                        .rearrange("(hc p) c -> p hc c", p=128),
                    ).then_inc(s_w[slot], 16)
                    if sb + 1 < NSB and q == max(0, NQ - 6):
                        loads(sb + 1)
                for blk in range(SBB):
                    gb = sb * SBB + blk
                    sync.wait_ge(s_o2f, (sb + 1) * NQ * n_o2_per_q)
                    if nxo > 0:
                        sync.wait_ge(s_xo, 16 * nxo)
                    sync.dma_start(
                        out=xb[:, :, :],
                        in_=x_in[tok0(sb, blk):tok0(sb, blk) + TBLK, :]
                        .rearrange("(tt p) c -> p tt c", p=128),
                    ).then_inc(s_xb, 16)
                    sync.wait_ge(s_xn, gb + 1)
                    sync.dma_start(
                        out=x_out[tok0(sb, blk):tok0(sb, blk) + TBLK, :]
                        .rearrange("(tt p) c -> p tt c", p=128),
                        in_=xb[:, :, :],
                    ).then_inc(s_xo, 16)
                    nxo += 1
            sync.wait_ge(s_xo, 16 * nxo)

        def body_scalar(s):
            NT = SBB * TPB

            def st_sq(sb, half):
                if half == 0:
                    s.wait_ge(s_xa, 16 * (sb + 1))
                for j in range(half * (NT // 2), (half + 1) * (NT // 2)):
                    s.drain()
                    s.activation(out=sqscr[:, :], in_=xa[:, j, :],
                                 func=AF.Square, bias=0.0, scale=1.0,
                                 accum_out=lnst[:, j, 1:2]).then_inc(s_ls, 1)

            def st_rstd(sb):
                s.wait_ge(s_lv, sb + 1)
                s.drain()
                s.activation(out=lnst[:, :, 3:4], in_=lnst[:, :, 3:4],
                             func=AF.Ln, bias=P["eps_col"][:, :], scale=1.0)
                s.drain()
                s.activation(out=lnst[:, :, 3:4], in_=lnst[:, :, 3:4],
                             func=AF.Exp, bias=0.0,
                             scale=-0.5).then_inc(s_lr, 1)

            def st_apply(sb):
                s.wait_ge(s_lm, sb + 1)
                for j in range(NT):
                    i = sb * NT + j
                    if i >= 2:
                        s.wait_ge(s_tp, i - 1)
                    s.activation(out=lntok2[:, i % 2, :], in_=xa[:, j, :],
                                 func=AF.Identity, scale=lnst[:, j, 3:4],
                                 bias=lnst[:, j, 4:5]).then_inc(s_la, 1)

            def st_gexp(sb):
                for blk in range(SBB):
                    gb = sb * SBB + blk
                    s.wait_ge(s_gm, gb + 1)
                    if gb > 0:
                        s.wait_ge(s_wt, gb * TPB)
                    s.drain()
                    s.activation(out=gateF[:, :], in_=gate_ps[0:E, :],
                                 func=AF.Exp, scale=-1.0,
                                 bias=P["negbg_col"][0:E, :]).then_inc(s_ge, 1)

            for sb in range(NSB):
                if sb == 0:
                    st_sq(0, 0)
                    st_sq(0, 1)
                    st_rstd(0)
                    st_apply(0)
                    st_gexp(0)
                for q in range(NQ):
                    L = sb * NQ + q
                    e, qi = q // QPE, q % QPE
                    for blk in range(SBB):
                        r = (sb * NQ + q) * SBB + blk
                        if r > 0:
                            s.wait_ge(s_o2m, r * TPB * ND)
                        for hc in range(HCQ):
                            gi = r * HCQ + hc
                            s.wait_ge(s_hmm, gi + 1)
                            s.activation(
                                out=h_sb[:, hc, :], in_=hps(gi),
                                func=AF.Relu,
                                bias=P["b1_cols"][:, e,
                                                  qi * HCQ + hc:
                                                  qi * HCQ + hc + 1],
                                scale=1.0,
                            ).then_inc(s_hev, 1)
                    if sb + 1 < NSB:
                        if q == max(0, NQ - 6):
                            st_sq(sb + 1, 0)
                        if q == max(0, NQ - 5):
                            st_sq(sb + 1, 1)
                        if q == max(1, NQ - 4):
                            st_rstd(sb + 1)
                        if q == max(2, NQ - 3):
                            st_apply(sb + 1)
                        if q == NQ - 1:
                            st_gexp(sb + 1)

        def body_vector(v):
            NT = SBB * TPB

            def st_vstat(sb, half):
                if half == 0:
                    v.wait_ge(s_xa, 16 * (sb + 1))
                    v.drain()
                    for j in range(NT // 2):
                        v.tensor_reduce(out=lnst[:, j, 0:1], in_=xa[:, j, :],
                                        axis=AX.X, op=ALU.add)
                    return
                for j in range(NT // 2, NT):
                    v.tensor_reduce(out=lnst[:, j, 0:1], in_=xa[:, j, :],
                                    axis=AX.X, op=ALU.add)
                v.wait_ge(s_ls, sb * NT + NT)
                v.drain()
                v.tensor_scalar(out=lnst[:, :, 2:3], in0=lnst[:, :, 0:1],
                                scalar1=Cinv, scalar2=None, op0=ALU.mult)
                v.tensor_scalar(out=lnst[:, :, 3:4], in0=lnst[:, :, 1:2],
                                scalar1=Cinv, scalar2=None, op0=ALU.mult)
                v.drain()
                v.tensor_tensor(out=lnst[:, :, 4:5], in0=lnst[:, :, 2:3],
                                in1=lnst[:, :, 2:3], op=ALU.mult)
                v.drain()
                v.tensor_tensor(out=lnst[:, :, 3:4], in0=lnst[:, :, 3:4],
                                in1=lnst[:, :, 4:5],
                                op=ALU.subtract).then_inc(s_lv, 1)

            def st_nmr(sb):
                v.wait_ge(s_lr, sb + 1)
                v.drain()
                v.tensor_tensor(out=lnst[:, :, 4:5], in0=lnst[:, :, 2:3],
                                in1=lnst[:, :, 3:4], op=ALU.mult)
                v.drain()
                v.tensor_scalar(out=lnst[:, :, 4:5], in0=lnst[:, :, 4:5],
                                scalar1=-1.0, scalar2=None,
                                op0=ALU.mult).then_inc(s_lm, 1)

            def st_evac(sb):
                par = sb % 2
                if sb > 1:
                    v.wait_ge(s_o2m, (sb - 1) * NQ * SBB * TPB * ND)
                for blk in range(SBB):
                    for tt in range(TPB):
                        i = (sb * SBB + blk) * TPB + tt
                        v.wait_ge(s_tp, i + 1)
                        v.tensor_copy(
                            out=ln_chan[:, par, blk, :,
                                        tt * 128:(tt + 1) * 128],
                            in_=tp_ps[:, :].rearrange("p (cc t) -> p cc t",
                                                      t=128),
                        ).then_inc(s_ln, 1)

            def pre_gatev(sb):
                par = sb % 2
                for blk in range(SBB):
                    gb = sb * SBB + blk
                    v.wait_ge(s_ge, gb + 1)
                    v.drain()
                    v.tensor_scalar(out=gateF[:, :], in0=gateF[:, :],
                                    scalar1=1.0, scalar2=None, op0=ALU.add)
                    v.drain()
                    v.reciprocal(out=gateF[:, :], in_=gateF[:, :])
                    v.drain()
                    v.tensor_scalar(out=maskF[:, :], in0=gateF[:, :],
                                    scalar1=0.5, scalar2=None, op0=ALU.is_gt)
                    v.drain()
                    v.scalar_tensor_tensor(out=gateF[:, :], in0=gateF[:, :],
                                           scalar=0.5, in1=maskF[:, :],
                                           op0=ALU.mult, op1=ALU.mult)
                    v.drain()
                    if sb > 0:
                        v.wait_ge(s_bm, (gb - SBB + 1) * TPB)
                    v.tensor_copy(out=w_ebf[:, blk, :],
                                  in_=gateF[:, :]).then_inc(s_gw, 1)
                    for tt in range(TPB):
                        i = gb * TPB + tt
                        v.wait_ge(s_wt, i + 1)
                        v.tensor_copy(out=w_tok[:, par, blk, tt, :],
                                      in_=bias_ps[:, 0:E]).then_inc(s_wv, 1)
                        v.wait_ge(s_bm, i + 1)
                        v.drain()
                        v.tensor_copy(out=comb[:, blk, tt, :],
                                      in_=bias_ps[:, :]).then_inc(s_bi, 1)


            for sb in range(NSB):
                par = sb % 2
                if sb == 0:
                    st_vstat(0, 0)
                    st_vstat(0, 1)
                    st_nmr(0)
                    st_evac(0)
                if sb > 0:
                    v.drain()
                pre_gatev(sb)
                for q in range(NQ):
                    e = q // QPE
                    for blk in range(SBB):
                        r = (sb * NQ + q) * SBB + blk
                        if blk == 0:
                            v.drain()
                        for tt in range(TPB):
                            for dh in range(ND):
                                gi = r * TPB * ND + tt * ND + dh
                                v.wait_ge(s_o2m, gi + 1)
                                v.scalar_tensor_tensor(
                                    out=comb[:, blk, tt,
                                             dh * DN:(dh + 1) * DN],
                                    in0=o2_ps[gi % 2][:, :],
                                    scalar=w_tok[:, par, blk, tt, e:e + 1],
                                    in1=comb[:, blk, tt,
                                             dh * DN:(dh + 1) * DN],
                                    op0=ALU.mult, op1=ALU.add,
                                ).then_inc(s_o2f, 1)
                    if sb + 1 < NSB:
                        if q == max(0, NQ - 6):
                            v.drain()
                            st_vstat(sb + 1, 0)
                        if q == max(0, NQ - 5):
                            v.drain()
                            st_vstat(sb + 1, 1)
                        if q == max(1, NQ - 4):
                            st_nmr(sb + 1)
                        if q == max(2, NQ - 3):
                            st_evac(sb + 1)
                for blk in range(SBB):
                    gb = sb * SBB + blk
                    v.wait_ge(s_xb, 16 * (gb + 1))
                    v.drain()
                    v.tensor_tensor(
                        out=xb[:, :, :].rearrange("p tt c -> p (tt c)"),
                        in0=comb[:, blk, :, :].rearrange("p tt c -> p (tt c)"),
                        in1=xb[:, :, :].rearrange("p tt c -> p (tt c)"),
                        op=ALU.add,
                    ).then_inc(s_xn, 1)

        def body_tensor(t):
            def st_tp(sb):
                for blk in range(SBB):
                    gb = sb * SBB + blk
                    for tt in range(TPB):
                        i = gb * TPB + tt
                        t.wait_ge(s_la, i + 1)
                        if i > 0:
                            t.wait_ge(s_ln, i)
                        for cc in range(CC):
                            mm = t.transpose(tp_ps[:, cc * 128:(cc + 1) * 128],
                                             in_=lntok2[:, i % 2,
                                                        cc * 128:
                                                        (cc + 1) * 128],
                                             identity=P["ident"][:, :])
                        mm.then_inc(s_tp, 1)

            def st_gmm(sb):
                par = sb % 2
                for blk in range(SBB):
                    gb = sb * SBB + blk
                    t.wait_ge(s_ln, (gb + 1) * TPB)
                    if gb > 0:
                        t.wait_ge(s_ge, gb)
                    for cc in range(CC):
                        mm = t.matmul(gate_ps[:, :], lhsT=P["wg_pad"][:, cc, :],
                                      rhs=ln_chan[:, par, blk, cc, :],
                                      start=(cc == 0), stop=(cc == CC - 1))
                    mm.then_inc(s_gm, 1)

            def st_wtr_bias(sb):
                if sb > 0:
                    t.wait_ge(s_hev, sb * NQ * SBB * HCQ)
                for blk in range(SBB):
                    gb = sb * SBB + blk
                    t.wait_ge(s_gw, gb + 1)
                    for tt in range(TPB):
                        i = gb * TPB + tt
                        if i > 0:
                            t.wait_ge(s_bi, i)
                        t.transpose(bias_ps[:, 0:E],
                                    in_=gateF[0:E, tt * 128:(tt + 1) * 128],
                                    identity=P["identF"][0:E, 0:E]
                                    ).then_inc(s_wt, 1)
                        t.wait_ge(s_wv, i + 1)
                        for dh in range(ND):
                            mm = t.matmul(
                                bias_ps[:, dh * DN:(dh + 1) * DN],
                                lhsT=w_ebf[:, blk, tt * 128:(tt + 1) * 128],
                                rhs=P["b2_bf"][:, dh * DN:(dh + 1) * DN],
                                start=True, stop=True)
                        mm.then_inc(s_bm, 1)

            for sb in range(NSB):
                par = sb % 2
                if sb == 0:
                    st_tp(0)
                    st_gmm(0)
                st_wtr_bias(sb)
                first_b2 = True
                for q in range(NQ):
                    L = sb * NQ + q
                    slot = L % 2
                    t.wait_ge(s_w[slot], 32 * (L // 2 + 1))
                    for blk in range(SBB):
                        r = (sb * NQ + q) * SBB + blk
                        for hc in range(HCQ):
                            gi = r * HCQ + hc
                            if gi >= 3:
                                t.wait_ge(s_hev, gi - 2)
                            if gi % 3 == 2 and first_b2:
                                t.wait_ge(s_bi, (sb + 1) * SBB * TPB)
                                first_b2 = False
                            for cc in range(CC):
                                mm = t.matmul(
                                    hps(gi),
                                    lhsT=w1q[:, slot, cc,
                                             hc * 128:(hc + 1) * 128],
                                    rhs=ln_chan[:, par, blk, cc, :],
                                    start=(cc == 0), stop=(cc == CC - 1))
                            mm.then_inc(s_hmm, 1)
                        t.wait_ge(s_hev, (r + 1) * HCQ)
                        for tt in range(TPB):
                            for dh in range(ND):
                                gi = r * TPB * ND + tt * ND + dh
                                if gi >= 2:
                                    t.wait_ge(s_o2f, gi - 1)
                                for hc in range(HCQ):
                                    mm = t.matmul(
                                        o2_ps[gi % 2][:, :],
                                        lhsT=h_sb[:, hc,
                                                  tt * 128:(tt + 1) * 128],
                                        rhs=w2q[:, slot, hc,
                                                dh * DN:(dh + 1) * DN],
                                        start=(hc == 0),
                                        stop=(hc == HCQ - 1))
                                mm.then_inc(s_o2m, 1)
                    if sb + 1 < NSB:
                        if q == max(2, NQ - 3):
                            st_tp(sb + 1)
                        if q == NQ - 1:
                            st_gmm(sb + 1)

        block = E_(nc.Block(label))
        block.sync(body_sync)
        block.scalar(body_scalar)
        block.vector(body_vector)
        block.tensor(body_tensor)


def emit_attn(nc, cfg, P, WT4, x_in, x_out, label="at"):
    C, S, T, TC = cfg.C, cfg.S, cfg.T, cfg.TC
    CC, G, NCH, ND, DN = cfg.CC, cfg.G, cfg.NCH, cfg.ND, cfg.DN
    Cinv = 1.0 / C

    with ExitStack() as ctx:
        E_ = ctx.enter_context
        xat = E_(nc.sbuf_tensor(f"{label}_xat", [128, 2, S, C], F32))
        UREP = P["urep"]
        WREP = P["wrep"]
        ln2 = E_(nc.sbuf_tensor(f"{label}_ln2", [128, G, TC + 1], BF16))
        xmx = E_(nc.sbuf_tensor(f"{label}_xmx", [128, 3, G, TC], BF16))
        dmx = E_(nc.sbuf_tensor(f"{label}_dmx", [128, G, TC], BF16))
        kb = E_(nc.sbuf_tensor(f"{label}_kb", [128, G, TC], F32))
        vb = E_(nc.sbuf_tensor(f"{label}_vb", [128, G, TC], F32))
        sg = E_(nc.sbuf_tensor(f"{label}_sg", [128, G, TC], F32))
        ku = E_(nc.sbuf_tensor(f"{label}_ku", [128, G, TC], F32))
        sc1 = E_(nc.sbuf_tensor(f"{label}_sc1", [128, G, TC], F32))
        sc2 = E_(nc.sbuf_tensor(f"{label}_sc2", [128, G, TC], F32))
        sc3 = E_(nc.sbuf_tensor(f"{label}_sc3", [128, G, TC], F32))
        ob = E_(nc.sbuf_tensor(f"{label}_ob", [128, G, TC + 1], F32))
        hb = E_(nc.sbuf_tensor(f"{label}_hb", [128, G, TC + 1], F32))
        qb = E_(nc.sbuf_tensor(f"{label}_qb", [128, G, TC + 1], F32))
        rwk = E_(nc.sbuf_tensor(f"{label}_rwk", [128, G, TC], BF16))
        lntok2 = E_(nc.sbuf_tensor(f"{label}_lntok", [128, 2, C], BF16))
        lnst = E_(nc.sbuf_tensor(f"{label}_lnst", [128, S, 5], F32))

        kv_ps = [E_(nc.psum_tensor(f"{label}_kvps{i}", [128, 4 * TC], F32))
                 for i in range(2)]
        tp_ps = E_(nc.psum_tensor(f"{label}_tpps", [128, C], BF16))
        wo_ps = [E_(nc.psum_tensor(f"{label}_wops{i}", [128, DN], F32))
                 for i in range(2)]

        s_ax = [E_(nc.semaphore(f"{label}_sax{i}")) for i in range(2)]
        s_ls = E_(nc.semaphore(f"{label}_sls"))
        s_lv = E_(nc.semaphore(f"{label}_slv"))
        s_lr = E_(nc.semaphore(f"{label}_slr"))
        s_lm = E_(nc.semaphore(f"{label}_slm"))
        s_la = E_(nc.semaphore(f"{label}_sla"))
        s_tp = E_(nc.semaphore(f"{label}_stp"))
        s_ln = E_(nc.semaphore(f"{label}_sln"))
        s_mx = E_(nc.semaphore(f"{label}_smx"))
        s_km = E_(nc.semaphore(f"{label}_skm"))
        s_rev = E_(nc.semaphore(f"{label}_srev"))
        s_vs = E_(nc.semaphore(f"{label}_svs"))
        s_ss = E_(nc.semaphore(f"{label}_sss"))
        s_sE = E_(nc.semaphore(f"{label}_ssE"))
        s_sA = E_(nc.semaphore(f"{label}_ssA"))
        s_rw = E_(nc.semaphore(f"{label}_srw"))
        s_vz = E_(nc.semaphore(f"{label}_svz"))
        s_sgf = E_(nc.semaphore(f"{label}_ssgf"))
        s_wm = E_(nc.semaphore(f"{label}_swm"))
        s_x2 = E_(nc.semaphore(f"{label}_sx2"))
        s_xo = E_(nc.semaphore(f"{label}_sxo"))

        WTk, WTv, WTr, WTo = WT4
        WT = {"k": WTk, "v": WTv, "r": WTr}
        TM = {"k": P["tmk_cols"], "v": P["tmv_cols"], "r": P["tmr_cols"]}
        NB4 = 3 * G // 4

        def body_sync(sync):
            def load(cb):
                for si in range(S):
                    sync.dma_start(
                        out=xat[:, cb % 2, si, :],
                        in_=x_in[si * T + cb * TC: si * T + (cb + 1) * TC, :],
                    ).then_inc(s_ax[cb % 2], 16)
            load(0)
            if NCH > 1:
                load(1)
            for cb in range(NCH):
                for si in range(S):
                    sync.wait_ge(s_x2, cb * S + si + 1)
                    sync.dma_start(
                        out=x_out[si * T + cb * TC: si * T + (cb + 1) * TC, :],
                        in_=xat[:, cb % 2, si, :],
                    ).then_inc(s_xo, 16)
                if cb + 2 < NCH:
                    sync.wait_ge(s_xo, 16 * S * (cb + 1))
                    load(cb + 2)
            sync.wait_ge(s_xo, 16 * S * NCH)

        def body_scalar(s):
            def sfrontA(cb):
                s.wait_ge(s_ax[cb % 2], 16 * S * (cb // 2 + 1))
                s.wait_ge(s_mx, cb * 3)
                sqscr = dmx[:, 0:C // TC, :].rearrange("p g t -> p (g t)")
                _ln_blk_scalar(s, P, [xat[:, cb % 2, si, :] for si in range(S)],
                               cb, S, sqscr, lnst, lntok2,
                               s_ls, s_lv, s_lr, s_lm, s_la, s_tp)
                s.drain()
                if cb == 0:
                    s.wait_ge(s_vz, 1)
                else:
                    s.activation(out=ln2[:, :, 0:1], in_=ln2[:, :, TC:TC + 1],
                                 func=AF.Identity, bias=0.0, scale=1.0)
                for si in range(S):
                    s.wait_ge(s_tp, cb * S + si + 1)
                    s.drain()
                    s.activation(
                        out=ln2[:, si * CC:(si + 1) * CC, 1:TC + 1],
                        in_=tp_ps[:, :].rearrange("p (cc t) -> p cc t", t=TC),
                        func=AF.Identity, bias=0.0,
                        scale=1.0).then_inc(s_ln, 1)

            def evac_batches(cb, lo, hi, guard=None):
                base = cb * 3 * G
                for b in range(lo, hi):
                    gi_last = base + b * 4 + 3
                    s.wait_ge(s_km, gi_last + 1)
                    if guard is not None:
                        s.wait_ge(*guard)
                        guard_used = True
                    bank = ((base + b * 4) // 4) % 2
                    q_of = (b * 4) // G
                    g0 = (b * 4) % G
                    if b == lo:
                        s.drain()
                    if q_of < 2:
                        buf = kb if q_of == 0 else vb
                        s.activation(out=buf[:, g0:g0 + 4, :],
                                     in_=kv_ps[bank][:, :], func=AF.Identity,
                                     bias=0.0, scale=1.0).then_inc(s_rev, 1)
                    else:
                        s.activation(out=sg[:, g0:g0 + 4, :],
                                     in_=kv_ps[bank][:, :], func=AF.Exp,
                                     bias=0.0, scale=-1.0).then_inc(s_rev, 1)

            def sigmoid_fin(cb):
                s.drain()
                s.activation(out=sg[:, :, :], in_=sg[:, :, :], func=AF.Ln,
                             bias=1.0, scale=1.0)
                s.drain()
                s.activation(out=sg[:, :, :], in_=sg[:, :, :], func=AF.Exp,
                             bias=0.0, scale=-1.0).then_inc(s_sgf, 1)

            def sback(cb, nxt):
                s.wait_ge(s_vs, cb * 3 + 1)
                s.drain()
                s.activation(out=sc1[:, :, :], in_=sc1[:, :, :], func=AF.Exp,
                             bias=0.0, scale=1.0).then_inc(s_sE, 1)
                s.activation(out=sc2[:, :, :], in_=sc2[:, :, :], func=AF.Exp,
                             bias=0.0, scale=1.0).then_inc(s_ss, 1)
                if nxt is not None:
                    evac_batches(nxt, 0, G // 4)
                s.wait_ge(s_vs, cb * 3 + 2)
                s.drain()
                s.activation(out=sc3[:, :, :], in_=sc3[:, :, :], func=AF.Exp,
                             bias=0.0, scale=1.0).then_inc(s_sA, 1)
                s.activation(out=ku[:, :, :], in_=ku[:, :, :], func=AF.Exp,
                             bias=0.0, scale=1.0).then_inc(s_ss, 1)
                s.wait_ge(s_vs, cb * 3 + 3)
                s.drain()
                s.activation(out=sc3[:, :, :], in_=sc3[:, :, :], func=AF.Ln,
                             bias=0.0, scale=1.0)
                s.drain()
                s.activation(out=sc3[:, :, :], in_=sc3[:, :, :], func=AF.Exp,
                             bias=0.0, scale=-1.0).then_inc(s_ss, 1)
                if nxt is not None:
                    evac_batches(nxt, G // 4, 2 * G // 4)
                    evac_batches(nxt, 2 * G // 4, NB4,
                                 guard=(s_rw, cb + 1))
                    sigmoid_fin(nxt)

            sfrontA(0)
            evac_batches(0, 0, NB4)
            sigmoid_fin(0)
            for cb in range(NCH):
                if cb + 1 < NCH:
                    sfrontA(cb + 1)
                sback(cb, cb + 1 if cb + 1 < NCH else None)

        def body_vector(v):
            def vfront(cb):
                par = cb % 2
                if cb == 0:
                    v.memset(ln2[:, :, 0:1], 0.0).then_inc(s_vz, 1)
                v.wait_ge(s_ax[cb % 2], 16 * S * (cb // 2 + 1))
                if cb > 0:
                    v.wait_ge(s_la, cb * S)
                _ln_blk_vector(v, Cinv,
                               [xat[:, cb % 2, si, :] for si in range(S)],
                               cb, S, lnst, s_ls, s_lv, s_lr, s_lm)
                v.wait_ge(s_ln, (cb + 1) * S)
                v.tensor_tensor(out=dmx[:, :, :], in0=ln2[:, :, 1:TC + 1],
                                in1=ln2[:, :, 0:TC], op=ALU.subtract)
                v.drain()
                for iq, qn in enumerate(("k", "v", "r")):
                    qidx = cb * 3 + iq
                    need = (qidx - 2) * G
                    if need > 0:
                        v.wait_ge(s_km, need)
                    last = None
                    for g in range(G):
                        cc = g % CC
                        last = v.scalar_tensor_tensor(
                            out=xmx[:, qidx % 3, g, :], in0=dmx[:, g, :],
                            scalar=TM[qn][:, cc:cc + 1],
                            in1=ln2[:, g, 0:TC],
                            op0=ALU.mult, op1=ALU.add)
                    last.then_inc(s_mx, 1)

            def vback(cb):
                v.wait_ge(s_rev, cb * NB4 + G // 4)
                v.drain()
                if cb == 0:
                    v.memset(ob[:, :, 0:1], NEG)
                    v.memset(hb[:, :, 0:1], 0.0)
                    v.memset(qb[:, :, 0:1], 0.0)
                else:
                    v.tensor_copy(out=ob[:, :, 0:1], in_=ob[:, :, TC:TC + 1])
                    v.tensor_copy(out=hb[:, :, 0:1], in_=hb[:, :, TC:TC + 1])
                    v.tensor_copy(out=qb[:, :, 0:1], in_=qb[:, :, TC:TC + 1])
                v.tensor_tensor(out=ku[:, :, :], in0=kb[:, :, :],
                                in1=UREP[:, :, :], op=ALU.add)
                v.drain()
                for g in range(G):
                    v.tensor_tensor_scan(out=ob[:, g, 1:TC + 1],
                                         data0=WREP[:, g, :],
                                         data1=kb[:, g, :],
                                         initial=ob[:, g, 0:1],
                                         op0=ALU.add, op1=ALU.max)
                v.drain()
                v.tensor_tensor(out=sc1[:, :, :], in0=kb[:, :, :],
                                in1=ob[:, :, 1:TC + 1], op=ALU.subtract)
                v.tensor_tensor(out=sc2[:, :, :], in0=ob[:, :, 0:TC],
                                in1=ob[:, :, 1:TC + 1], op=ALU.subtract)
                v.drain()
                v.tensor_tensor(out=sc2[:, :, :], in0=sc2[:, :, :],
                                in1=WREP[:, :, :],
                                op=ALU.add).then_inc(s_vs, 1)
                v.wait_ge(s_sE, cb + 1)
                v.wait_ge(s_rev, cb * NB4 + 2 * (G // 4))
                v.tensor_tensor(out=sc3[:, :, :], in0=sc1[:, :, :],
                                in1=vb[:, :, :], op=ALU.mult)
                v.wait_ge(s_ss, cb * 3 + 1)
                v.drain()
                for g in range(G):
                    v.tensor_tensor_scan(out=hb[:, g, 1:TC + 1],
                                         data0=sc2[:, g, :],
                                         data1=sc3[:, g, :],
                                         initial=hb[:, g, 0:1],
                                         op0=ALU.mult, op1=ALU.add)
                for g in range(G):
                    v.tensor_tensor_scan(out=qb[:, g, 1:TC + 1],
                                         data0=sc2[:, g, :],
                                         data1=sc1[:, g, :],
                                         initial=qb[:, g, 0:1],
                                         op0=ALU.mult, op1=ALU.add)
                v.drain()
                v.tensor_tensor(out=sc1[:, :, :], in0=ob[:, :, 0:TC],
                                in1=ku[:, :, :], op=ALU.max)
                v.drain()
                v.tensor_tensor(out=sc3[:, :, :], in0=ob[:, :, 0:TC],
                                in1=sc1[:, :, :], op=ALU.subtract)
                v.tensor_tensor(out=ku[:, :, :], in0=ku[:, :, :],
                                in1=sc1[:, :, :],
                                op=ALU.subtract).then_inc(s_vs, 1)
                v.wait_ge(s_sA, cb + 1)
                v.tensor_tensor(out=sc2[:, :, :], in0=sc3[:, :, :],
                                in1=hb[:, :, 0:TC], op=ALU.mult)
                v.wait_ge(s_ss, cb * 3 + 2)
                v.tensor_tensor(out=vb[:, :, :], in0=ku[:, :, :],
                                in1=vb[:, :, :], op=ALU.mult)
                v.tensor_tensor(out=sc3[:, :, :], in0=sc3[:, :, :],
                                in1=qb[:, :, 0:TC], op=ALU.mult)
                v.drain()
                v.tensor_tensor(out=sc2[:, :, :], in0=sc2[:, :, :],
                                in1=vb[:, :, :], op=ALU.add)
                v.tensor_tensor(out=sc3[:, :, :], in0=sc3[:, :, :],
                                in1=ku[:, :, :],
                                op=ALU.add).then_inc(s_vs, 1)
                v.wait_ge(s_ss, cb * 3 + 3)
                v.drain()
                v.tensor_tensor(out=sc2[:, :, :], in0=sc2[:, :, :],
                                in1=sc3[:, :, :], op=ALU.mult)
                v.wait_ge(s_sgf, cb + 1)
                v.drain()
                v.tensor_tensor(out=rwk[:, :, :], in0=sc2[:, :, :],
                                in1=sg[:, :, :], op=ALU.mult).then_inc(s_rw, 1)
                for si in range(S):
                    last = None
                    for dh in range(ND):
                        gi = cb * S * ND + si * ND + dh
                        v.wait_ge(s_wm, gi + 1)
                        last = v.tensor_tensor(
                            out=xat[:, cb % 2, si, dh * DN:(dh + 1) * DN],
                            in0=wo_ps[gi % 2][:, :],
                            in1=xat[:, cb % 2, si, dh * DN:(dh + 1) * DN],
                            op=ALU.add)
                    last.then_inc(s_x2, 1)

            vfront(0)
            for cb in range(NCH):
                if cb + 1 < NCH:
                    vfront(cb + 1)
                vback(cb)

        def body_tensor(t):
            def tfront(cb):
                par = cb % 2
                for si in range(S):
                    i = cb * S + si
                    t.wait_ge(s_la, i + 1)
                    if i > 0:
                        t.wait_ge(s_ln, i)
                    for cc in range(CC):
                        mm = t.transpose(tp_ps[:, cc * TC:(cc + 1) * TC],
                                         in_=lntok2[:, i % 2,
                                                    cc * 128:(cc + 1) * 128],
                                         identity=P["ident"][:, :])
                    mm.then_inc(s_tp, 1)
                base = cb * 3 * G
                for iq, qn in enumerate(("k", "v", "r")):
                    qidx = cb * 3 + iq
                    t.wait_ge(s_mx, qidx + 1)
                    for si in range(S):
                        for dc in range(CC):
                            gi = base + iq * G + si * CC + dc
                            bank, slot = (gi // 4) % 2, gi % 4
                            if gi >= 8:
                                t.wait_ge(s_rev, (gi - 8) // 4 + 1)
                            for cc in range(CC):
                                mm = t.matmul(
                                    kv_ps[bank][:, slot * TC:(slot + 1) * TC],
                                    lhsT=WT[qn][:, cc,
                                                dc * 128:(dc + 1) * 128],
                                    rhs=xmx[:, qidx % 3, si * CC + cc, :],
                                    start=(cc == 0), stop=(cc == CC - 1))
                            mm.then_inc(s_km, 1)

            def tback(cb):
                t.wait_ge(s_rw, cb + 1)
                for si in range(S):
                    for dh in range(ND):
                        gi = cb * S * ND + si * ND + dh
                        if gi >= 2:
                            prev_si = ((gi - 2) % (S * ND)) // ND
                            prev_cb = (gi - 2) // (S * ND)
                            t.wait_ge(s_x2, prev_cb * S + prev_si + 1)
                        for cc in range(CC):
                            mm = t.matmul(
                                wo_ps[gi % 2][:, :],
                                lhsT=rwk[:, si * CC + cc, :],
                                rhs=WTo[:, cc, dh * DN:(dh + 1) * DN],
                                start=(cc == 0), stop=(cc == CC - 1))
                        mm.then_inc(s_wm, 1)

            tfront(0)
            for cb in range(NCH):
                if cb + 1 < NCH:
                    tfront(cb + 1)
                tback(cb)

        block = E_(nc.Block(label))
        block.sync(body_sync)
        block.scalar(body_scalar)
        block.vector(body_vector)
        block.tensor(body_tensor)


def emit_prologue(nc, cfg, P, ins, w1c, w2c, s_gx, s_gE):
    C, E, H, CC, TC = cfg.C, cfg.E, cfg.H, cfg.CC, cfg.TC
    HCALL = cfg.HCALL

    with ExitStack() as ctx:
        E_ = ctx.enter_context
        b1raw = E_(nc.sbuf_tensor("p0_b1raw", [E, H], F32))
        b2raw = E_(nc.sbuf_tensor("p0_b2raw", [E, C], F32))
        smallT = E_(nc.sbuf_tensor("p0_small", [CC, 128 * 5], F32))
        pp = E_(nc.psum_tensor("p0_pp", [128, 128], F32))
        s_d = E_(nc.semaphore("p0_sd"))
        s_g = s_gx
        s_t = E_(nc.semaphore("p0_st"))
        s_v = E_(nc.semaphore("p0_sv"))
        s_s = E_(nc.semaphore("p0_ss"))

        n_tp = HCALL + 5
        small_names = ["td", "tf", "tmk", "tmv", "tmr"]
        col_targets = [P["w_cols"], P["u_cols"], P["tmk_cols"], P["tmv_cols"],
                       P["tmr_cols"]]

        def body_sync(sync):
            sync.dma_start(out=P["ident"][:, :],
                           in_=ins["ident"][:, :]).then_inc(s_d, 16)
            sync.dma_start(out=P["identF"][:, :],
                           in_=ins["identF"][:, :]).then_inc(s_d, 16)
            sync.dma_start(out=b1raw[:, :], in_=ins["b1"][:, :]).then_inc(s_d, 16)
            sync.dma_start(out=b2raw[:, :], in_=ins["b2"][:, :]).then_inc(s_d, 16)
            sync.dma_start(out=P["negbg_col"][0:E, :],
                           in_=ins["bg"][:].rearrange("(e u) -> e u", u=1)
                           ).then_inc(s_d, 16)
            for j, nm in enumerate(small_names):
                sync.dma_start(out=smallT[:, j * 128:(j + 1) * 128],
                               in_=ins[nm][:].rearrange("(cc p) -> cc p", p=128)
                               ).then_inc(s_d, 16)

        def body_gpsimd(g):
            g.wait_ge(s_v, 1)
            g.dma_start(out=P["wg_pad"][:, :, 0:E],
                        in_=ins["wg"][:, :].rearrange("(cc p) e -> p cc e",
                                                      p=128)).then_inc(s_g, 16)
            for e in range(E):
                g.dma_start(out=w1c[e, :, :], in_=ins["w1"][e, :, :],
                            max_dma_last_dim=8192).then_inc(s_gE[e], 16)
                g.dma_start(out=w2c[e, :, :], in_=ins["w2"][e, :, :],
                            max_dma_last_dim=8192).then_inc(s_gE[e], 16)
            g.wait_ge(s_g, 16)

        def body_tensor(t):
            t.wait_ge(s_d, 16 * (5 + len(small_names)))
            for hq in range(HCALL):
                if hq > 0:
                    t.wait_ge(s_v, 2 + hq)
                t.transpose(pp[:, 0:E], in_=b1raw[:, hq * 128:(hq + 1) * 128],
                            identity=P["identF"][0:E, 0:E]).then_inc(s_t, 1)
            for j in range(5):
                t.wait_ge(s_v, 2 + HCALL + j)
                t.transpose(pp[:, 0:CC], in_=smallT[:, j * 128:(j + 1) * 128],
                            identity=P["identF"][0:CC, 0:CC]).then_inc(s_t, 1)

        def body_vector(v):
            v.memset(P["wg_pad"][:, :, :], 0.0)
            v.drain()
            v.memset(P["eps_col"][:, :], 1e-5).then_inc(s_v, 1)
            v.wait_ge(s_d, 16 * (5 + len(small_names)))
            v.tensor_scalar(out=P["negbg_col"][0:E, :],
                            in0=P["negbg_col"][0:E, :], scalar1=-1.0,
                            scalar2=None, op0=ALU.mult)
            v.drain()
            v.tensor_copy(out=P["b2_bf"][:, :], in_=b2raw[:, :]).then_inc(s_v, 1)
            for hq in range(HCALL):
                v.wait_ge(s_t, hq + 1)
                v.tensor_copy(
                    out=P["b1_cols"][:, :, hq:hq + 1]
                    .rearrange("p e u -> p (e u)"),
                    in_=pp[:, 0:E]).then_inc(s_v, 1)
            for j in range(5):
                v.wait_ge(s_t, HCALL + j + 1)
                v.tensor_copy(out=col_targets[j][:, :],
                              in_=pp[:, 0:CC]).then_inc(s_v, 1)
            v.wait_ge(s_s, 1)
            v.drain()
            v.tensor_scalar(out=P["w_cols"][:, :], in0=P["w_cols"][:, :],
                            scalar1=-1.0, scalar2=None, op0=ALU.mult)
            v.drain()

        def body_scalar(s):
            s.wait_ge(s_v, 2 + HCALL + 1)
            s.drain()
            s.activation(out=P["w_cols"][:, :], in_=P["w_cols"][:, :],
                         func=AF.Exp, bias=0.0, scale=1.0).then_inc(s_s, 1)

        block = E_(nc.Block("p0", no_gpsimd_drain=True))
        block.sync(body_sync)
        block.gpsimd(body_gpsimd)
        block.tensor(body_tensor)
        block.vector(body_vector)
        block.scalar(body_scalar)


def emit_wtrans(nc, cfg, P, ins, WT4):
    C, CC = cfg.C, cfg.CC
    names = ["wk", "wv", "wr", "wo"]
    with ExitStack() as ctx:
        E_ = ctx.enter_context
        wtmp = E_(nc.sbuf_tensor("pw_wtmp", [128, C], BF16))
        pw_ps = E_(nc.psum_tensor("pw_ps", [128, C], BF16))
        s_g = E_(nc.semaphore("pw_sg"))
        s_t = E_(nc.semaphore("pw_st"))
        s_v = E_(nc.semaphore("pw_sv"))

        def body_gpsimd(g):
            nd = 0
            for wi in range(4):
                for dc in range(CC):
                    if nd > 0:
                        g.wait_ge(s_t, nd)
                    g.dma_start(out=wtmp[:, :],
                                in_=ins[names[wi]][dc * 128:(dc + 1) * 128, :]
                                ).then_inc(s_g, 16)
                    nd += 1

        def body_tensor(t):
            nd = 0
            for wi in range(4):
                for dc in range(CC):
                    t.wait_ge(s_g, 16 * (nd + 1))
                    if nd > 0:
                        t.wait_ge(s_v, nd)
                    for cc in range(CC):
                        mm = t.transpose(pw_ps[:, cc * 128:(cc + 1) * 128],
                                         in_=wtmp[:, cc * 128:(cc + 1) * 128],
                                         identity=P["ident"][:, :])
                    mm.then_inc(s_t, 1)
                    nd += 1

        def body_vector(v):
            G = cfg.G
            v.memset(P["urep"][:, :, :], 0.0)
            v.drain()
            v.memset(P["wrep"][:, :, :], 0.0)
            v.drain()
            for g in range(G):
                v.tensor_scalar(out=P["urep"][:, g, :], in0=P["urep"][:, g, :],
                                scalar1=P["u_cols"][:, g % CC:g % CC + 1],
                                scalar2=None, op0=ALU.add)
            for g in range(G):
                v.tensor_scalar(out=P["wrep"][:, g, :], in0=P["wrep"][:, g, :],
                                scalar1=P["w_cols"][:, g % CC:g % CC + 1],
                                scalar2=None, op0=ALU.add)
            v.drain()
            nd = 0
            for wi in range(4):
                for dc in range(CC):
                    v.wait_ge(s_t, nd + 1)
                    v.tensor_copy(
                        out=WT4[wi][:, :, dc * 128:(dc + 1) * 128],
                        in_=pw_ps[:, :].rearrange("p (cc d) -> p cc d", d=128),
                    ).then_inc(s_v, 1)
                    nd += 1

        block = E_(nc.Block("pw"))
        block.gpsimd(body_gpsimd)
        block.tensor(body_tensor)
        block.vector(body_vector)


def build_program(cfg):
    nc = bass.Bass("TRN2", target_bir_lowering=False)
    C, E, H, TOK = cfg.C, cfg.E, cfg.H, cfg.TOK
    ins = {
        "x": nc.dram_tensor("x", [TOK, C], F32, kind="ExternalInput").ap(),
        "w1": nc.dram_tensor("w1", [E, C, H], F32, kind="ExternalInput").ap(),
        "w2": nc.dram_tensor("w2", [E, H, C], F32, kind="ExternalInput").ap(),
        "wk": nc.dram_tensor("wk", [C, C], F32, kind="ExternalInput").ap(),
        "wv": nc.dram_tensor("wv", [C, C], F32, kind="ExternalInput").ap(),
        "wr": nc.dram_tensor("wr", [C, C], F32, kind="ExternalInput").ap(),
        "wo": nc.dram_tensor("wo", [C, C], F32, kind="ExternalInput").ap(),
        "wg": nc.dram_tensor("wg", [C, E], F32, kind="ExternalInput").ap(),
        "bg": nc.dram_tensor("bg", [E], F32, kind="ExternalInput").ap(),
        "b1": nc.dram_tensor("b1", [E, H], F32, kind="ExternalInput").ap(),
        "b2": nc.dram_tensor("b2", [E, C], F32, kind="ExternalInput").ap(),
        "td": nc.dram_tensor("td", [C], F32, kind="ExternalInput").ap(),
        "tf": nc.dram_tensor("tf", [C], F32, kind="ExternalInput").ap(),
        "tmk": nc.dram_tensor("tmk", [C], F32, kind="ExternalInput").ap(),
        "tmv": nc.dram_tensor("tmv", [C], F32, kind="ExternalInput").ap(),
        "tmr": nc.dram_tensor("tmr", [C], F32, kind="ExternalInput").ap(),
        "ident": nc.dram_tensor("ident", [128, 128], BF16,
                                kind="ExternalInput").ap(),
        "identF": nc.dram_tensor("identF", [128, 128], F32,
                                 kind="ExternalInput").ap(),
    }
    out = nc.dram_tensor("out", [TOK, C], F32, kind="ExternalOutput").ap()
    x1 = nc.dram_tensor("x1s", [TOK, C], F32, kind="Internal").ap()
    x2 = nc.dram_tensor("x2s", [TOK, C], F32, kind="Internal").ap()
    w1c = nc.dram_tensor("w1c", [E, C, H], BF16, kind="Internal").ap()
    w2c = nc.dram_tensor("w2c", [E, H, C], BF16, kind="Internal").ap()

    with ExitStack() as big:
        B_ = big.enter_context
        s_cache = B_(nc.semaphore("s_cache"))
        s_cacheE = [B_(nc.semaphore(f"s_cachee{e}")) for e in range(E)]
        P = {
            "ident": B_(nc.sbuf_tensor("P_ident", [128, 128], BF16)),
            "identF": B_(nc.sbuf_tensor("P_identF", [128, 128], F32)),
            "wg_pad": B_(nc.sbuf_tensor("P_wgpad", [128, cfg.CC, 128], BF16)),
            "negbg_col": B_(nc.sbuf_tensor("P_negbg", [128, 1], F32)),
            "b1_cols": B_(nc.sbuf_tensor("P_b1c", [128, E, cfg.HCALL], F32)),
            "b2_bf": B_(nc.sbuf_tensor("P_b2bf", [E, C], BF16)),
            "eps_col": B_(nc.sbuf_tensor("P_eps", [128, 1], F32)),
            "w_cols": B_(nc.sbuf_tensor("P_wcol", [128, cfg.CC], F32)),
            "u_cols": B_(nc.sbuf_tensor("P_ucol", [128, cfg.CC], F32)),
            "tmk_cols": B_(nc.sbuf_tensor("P_tmk", [128, cfg.CC], F32)),
            "tmv_cols": B_(nc.sbuf_tensor("P_tmv", [128, cfg.CC], F32)),
            "tmr_cols": B_(nc.sbuf_tensor("P_tmr", [128, cfg.CC], F32)),
        }
        emit_prologue(nc, cfg, P, ins, w1c, w2c, s_cache, s_cacheE)
        emit_ffn(nc, cfg, P, ins["x"], x1, w1c, w2c, label="f1",
                 cache_sem=s_cacheE)
        with ExitStack() as attn_ctx:
            A_ = attn_ctx.enter_context
            WT4 = [A_(nc.sbuf_tensor(f"WT{i}", [128, cfg.CC, C], BF16))
                   for i in range(4)]
            P["urep"] = A_(nc.sbuf_tensor("P_urep", [128, cfg.G, cfg.TC], BF16))
            P["wrep"] = A_(nc.sbuf_tensor("P_wrep", [128, cfg.G, cfg.TC], F32))
            emit_wtrans(nc, cfg, P, ins, WT4)
            emit_attn(nc, cfg, P, WT4, x1, x2, label="at")
        emit_ffn(nc, cfg, P, x2, out, w1c, w2c, label="f2")
    return nc


def kernel(**inputs):
    import ml_dtypes
    cfg = Cfg()
    n_cores = 8
    B = 16
    x = np.ascontiguousarray(inputs["L0_attn_x"], dtype=np.float32)
    shared = {
        "w1": np.ascontiguousarray(inputs["L0_ffn_W1"], dtype=np.float32),
        "w2": np.ascontiguousarray(inputs["L0_ffn_W2"], dtype=np.float32),
        "wk": np.ascontiguousarray(inputs["L0_attn_Wk"], dtype=np.float32),
        "wv": np.ascontiguousarray(inputs["L0_attn_Wv"], dtype=np.float32),
        "wr": np.ascontiguousarray(inputs["L0_attn_Wr"], dtype=np.float32),
        "wo": np.ascontiguousarray(inputs["L0_attn_Wo"], dtype=np.float32),
        "wg": np.ascontiguousarray(inputs["L0_ffn_Wg"], dtype=np.float32),
        "bg": np.ascontiguousarray(inputs["L0_ffn_bg"], dtype=np.float32),
        "b1": np.ascontiguousarray(inputs["L0_ffn_b1"], dtype=np.float32),
        "b2": np.ascontiguousarray(inputs["L0_ffn_b2"], dtype=np.float32),
        "td": np.ascontiguousarray(inputs["L0_attn_time_decay"],
                                   dtype=np.float32),
        "tf": np.ascontiguousarray(inputs["L0_attn_time_first"],
                                   dtype=np.float32),
        "tmk": np.ascontiguousarray(inputs["L0_attn_time_mix_k"],
                                    dtype=np.float32).reshape(-1),
        "tmv": np.ascontiguousarray(inputs["L0_attn_time_mix_v"],
                                    dtype=np.float32).reshape(-1),
        "tmr": np.ascontiguousarray(inputs["L0_attn_time_mix_r"],
                                    dtype=np.float32).reshape(-1),
        "ident": np.eye(128, dtype=ml_dtypes.bfloat16),
        "identF": np.eye(128, dtype=np.float32),
    }
    spc = B // n_cores
    in_maps = []
    for i in range(n_cores):
        m = dict(shared)
        m["x"] = np.ascontiguousarray(
            x[i * spc:(i + 1) * spc].reshape(cfg.TOK, cfg.C))
        in_maps.append(m)
    nc = build_program(cfg)
    res = run_bass_kernel_spmd(nc, in_maps, core_ids=list(range(n_cores)))
    parts = [res.results[i]["out"].reshape(spc, cfg.T, cfg.C)
             for i in range(n_cores)]
    return np.ascontiguousarray(np.concatenate(parts, axis=0),
                                dtype=np.float32)
